# version 01 (final)
# baseline (speedup 1.0000x reference)
import math
from contextlib import ExitStack

import numpy as np
import ml_dtypes

import concourse.bass as bass
import concourse.mybir as mybir
from concourse.bass_utils import run_bass_kernel_spmd

F32 = mybir.dt.float32
BF16 = mybir.dt.bfloat16
AF = mybir.ActivationFunctionType
ALU = mybir.AluOpType
AX = mybir.AxisListType

BF = ml_dtypes.bfloat16
NCORE = 8
T = 2048
NT = T // 128
C = 1024
LAMBDA_INIT = 0.8 - 0.6 * math.exp(-0.3 * 12)

ENGS = ("tensor", "vector", "scalar", "sync", "gpsimd")


class Bld:
    def __init__(self, nc, st):
        self.nc = nc
        self.st = st
        self.items = []
        self.cnt = {e: 0 for e in ENGS}
        self.waited = {}
        self.prev = None
        self.sb_n = 0

    def sbuf(self, shape, dtype):
        self.sb_n += 1
        return self.st.enter_context(self.nc.sbuf_tensor(f"sb{self.sb_n}", shape, dtype))

    def psum(self, shape, dtype):
        self.sb_n += 1
        return self.st.enter_context(self.nc.psum_tensor(f"ps{self.sb_n}", shape, dtype))

    def _wait(self, eng, tok):
        src, val = tok
        if val <= 0:
            return
        key = (eng, src)
        if self.waited.get(key, 0) >= val:
            return
        self.waited[key] = val
        self.items.append(("w", eng, src, val))

    def op(self, eng, fn, deps="p"):
        if deps == "p":
            deps = [self.prev] if self.prev else []
        elif deps is None:
            deps = []
        for d in deps:
            if d:
                self._wait(eng, d)
        step = 16 if eng in ("sync", "gpsimd") else 1
        self.cnt[eng] += step
        self.items.append(("i", eng, fn, step))
        tok = (eng, self.cnt[eng])
        self.prev = tok
        return tok

    def t(self, fn, deps="p"):
        return self.op("tensor", fn, deps)

    def v(self, fn, deps="p"):
        return self.op("vector", fn, deps)

    def s(self, fn, deps="p"):
        return self.op("scalar", fn, deps)

    def ld(self, out, in_, deps="p", slow=False):
        return self.op("sync", lambda e, o=out, i=in_, s=slow: e.dma_start(out=o, in_=i, allow_slow_non_contiguous=s), deps)

    def stdma(self, out, in_, deps="p"):
        return self.op("gpsimd", lambda e, o=out, i=in_: e.dma_start(out=o, in_=i), deps)

    def mm(self, out, lhsT, rhs, first=True, last=True):
        return self.t(
            lambda e, o=out, l=lhsT, r=rhs, s=first, p=last: e.matmul(o, l, r, start=s, stop=p),
            deps="p" if first else None,
        )

    def tr(self, out, in_, ident):
        return self.t(lambda e, o=out, i=in_, d=ident: e.transpose(o, i, d))

    def copy(self, out, in_, eng="vector"):
        return self.op(eng, lambda e, o=out, i=in_: e.tensor_copy(o, i) if eng != "scalar" else e.copy(o, i))

    def finish(self):
        self._wait("sync", ("gpsimd", self.cnt["gpsimd"]))
        for e in ("tensor", "vector", "scalar"):
            self._wait("sync", (e, self.cnt[e]))

    def play(self):
        nc = self.nc
        sems = {e: self.st.enter_context(nc.semaphore(f"sem_{e}")) for e in ENGS}
        blk = self.st.enter_context(nc.Block())
        for e in ENGS:
            def section(engobj, e=e):
                for it in self.items:
                    if it[1] != e:
                        continue
                    if it[0] == "w":
                        engobj.wait_ge(sems[it[2]], it[3])
                    else:
                        it[2](engobj).then_inc(sems[e], it[3])
            getattr(blk, e)(section)


def build_program(emit, io, name):
    nc = bass.Bass("TRN2", target_bir_lowering=False)
    A = {}
    for nm, (shape, dtype, kind) in io.items():
        A[nm] = nc.dram_tensor(nm, list(shape), dtype, kind=kind).ap()
    with ExitStack() as st:
        bld = Bld(nc, st)
        emit(bld, A)
        bld.finish()
        bld.play()
    n_ins = sum(1 for it in bld.items)
    print(f"[kernel] program {name}: {n_ins} items, cnts={bld.cnt}")
    return nc



def _consts():
    cs = {}
    cs["ident_b"] = np.eye(128, dtype=np.float32).astype(BF)
    cs["ident_f"] = np.eye(128, dtype=np.float32)
    tril01 = np.tril(np.ones((128, 128), dtype=np.float32))
    cs["tril01"] = tril01
    cs["ones_row_f"] = np.ones((1, 128), dtype=np.float32)
    cs["ones_row_b"] = np.ones((1, 128), dtype=np.float32).astype(BF)
    cs["ones_col_f"] = np.ones((128, 1), dtype=np.float32)
    cs["ones_col_b"] = np.ones((128, 1), dtype=np.float32).astype(BF)
    d = 64
    inv_freq = 1.0 / (10000.0 ** (np.arange(0, d, 2, dtype=np.float32) / d))
    tpos = np.arange(T, dtype=np.float32)
    fr = tpos[:, None] * inv_freq[None, :]
    cs["cos1"] = np.tile(np.cos(fr), (1, 16)).astype(np.float32)
    cs["sin1"] = np.tile(np.sin(fr), (1, 16)).astype(np.float32)
    f3 = 1.0 / (10000.0 ** (np.arange(0, 64, 2, dtype=np.float32) / 64))
    a3 = tpos[:, None] * f3[None, :]
    cs["cos3"] = np.tile(np.cos(a3), (1, 16)).astype(np.float32)
    cs["sin3"] = np.tile(np.sin(a3), (1, 16)).astype(np.float32)
    cs["iota_bc"] = np.tile(np.arange(256, dtype=np.float32)[None, :], (128, 1))
    ts = (np.arange(T)[:, None] < np.arange(T)[None, :]).astype(np.float32)
    cs["tsmask"] = ts.astype(BF)
    return cs


CONSTS = _consts()

CONST_IO = {
    "ident_b": ((128, 128), BF16),
    "ident_f": ((128, 128), F32),
    "tril01": ((128, 128), F32),
    "ones_row_f": ((1, 128), F32),
    "ones_row_b": ((1, 128), BF16),
    "ones_col_f": ((128, 1), F32),
    "ones_col_b": ((128, 1), BF16),
    "cos1": ((T, 512), F32),
    "sin1": ((T, 512), F32),
    "cos3": ((T, 512), F32),
    "sin3": ((T, 512), F32),
    "iota_bc": ((128, 256), F32),
    "tsmask": ((T, T), BF16),
}


def io_inputs(names):
    return {n: (CONST_IO[n][0], CONST_IO[n][1], "ExternalInput") for n in names}



class Common:

    def __init__(self, bld, A):
        self.bld = bld
        self.A = A
        b = bld
        self.ps_s = b.psum([128, 2048], F32)
        self.ps0 = b.psum([128, 512], F32)
        self.ps1 = b.psum([128, 512], F32)
        self.ps2 = b.psum([128, 512], F32)
        self.ps_tb = b.psum([128, 1024], BF16)
        self.xt = b.sbuf([128, 1024], F32)
        self.xc = b.sbuf([128, 1024], F32)
        self.sq = b.sbuf([128, 1024], F32)
        self.xh_b = b.sbuf([128, 1024], BF16)
        self.col0 = b.sbuf([128, 1], F32)
        self.col1 = b.sbuf([128, 1], F32)
        self.col2 = b.sbuf([128, 1], F32)
        self.col3 = b.sbuf([128, 1], F32)
        self.ysb = b.sbuf([128, 1024], F32)
        self.idb = b.sbuf([128, 128], BF16)
        self.idf = b.sbuf([128, 128], F32)
        self.tril = b.sbuf([128, 128], F32)
        self.orow_f = b.sbuf([1, 128], F32)
        self.ocol_f = b.sbuf([128, 1], F32)
        self.ocol_b = b.sbuf([128, 1], BF16)
        b.ld(self.idb[:, :], A["ident_b"][:, :])
        b.ld(self.idf[:, :], A["ident_f"][:, :])
        b.ld(self.tril[:, :], A["tril01"][:, :])
        b.ld(self.orow_f[:, :], A["ones_row_f"][:, :])
        b.ld(self.ocol_f[:, :], A["ones_col_f"][:, :])
        b.ld(self.ocol_b[:, :], A["ones_col_b"][:, :])
        self.c_eps5 = b.sbuf([128, 1], F32)
        self.c_eps6 = b.sbuf([128, 1], F32)
        self.c_eps8 = b.sbuf([128, 1], F32)
        b.v(lambda e: e.memset(self.c_eps5[:], 1e-5))
        b.v(lambda e: e.memset(self.c_eps6[:], 1e-6))
        b.v(lambda e: e.memset(self.c_eps8[:], 1e-8))

    def ln_tile(self, xin, ti, out_b=None, keep_x=True):
        b = self.bld
        if out_b is None:
            out_b = self.xh_b
        b.ld(self.xt[:], xin[ti * 128:(ti + 1) * 128, :])
        b.v(lambda e, o=self.col0, i=self.xt: e.reduce_sum(o[:], i[:], axis=AX.X))
        b.v(lambda e, o=self.col1, i=self.col0: e.tensor_scalar_mul(o[:], i[:], -1.0 / 1024.0))
        b.v(lambda e, o=self.xc, i=self.xt, s=self.col1: e.tensor_scalar_add(o[:], i[:], s[:]))
        b.v(lambda e, o=self.sq, i=self.xc: e.tensor_mul(o[:], i[:], i[:]))
        b.v(lambda e, o=self.col2, i=self.sq: e.reduce_sum(o[:], i[:], axis=AX.X))
        b.s(lambda e, o=self.col3, i=self.col2: e.activation(o[:], i[:], AF.Sqrt, bias=self.c_eps5[:], scale=1.0 / 1024.0))
        b.v(lambda e, o=self.col2, i=self.col3: e.reciprocal(o[:], i[:]))
        b.v(lambda e, o=out_b, i=self.xc, s=self.col2: e.tensor_scalar_mul(o[:], i[:], s[:]))
        return out_b

    def tile_to_fmajor(self, src_b, dst, ti, nch=8, width=T):
        b = self.bld
        for c in range(nch):
            b.tr(self.ps_tb[:, c * 128:(c + 1) * 128], src_b[:, c * 128:(c + 1) * 128], self.idb[:])
        dv = dst[:, :].rearrange("p (c t) -> p c t", c=nch)[:, :, ti * 128:(ti + 1) * 128]
        sv = self.ps_tb[:, 0:nch * 128].rearrange("p (c t) -> p c t", c=nch)
        b.v(lambda e, o=dv, i=sv: e.tensor_copy(o, i))

    def softmax(self, kvalid, P_sb, masked_from=None, scale=1.0, rs_out=None):
        b = self.bld
        A = self.A
        b.v(lambda e, o=self.col0, i=self.ps_s, k=kvalid: e.tensor_reduce(o[:], i[:, 0:k], axis=AX.X, op=ALU.max))
        b.v(lambda e, o=self.col0, s=scale: e.tensor_scalar_mul(o[:], o[:], -s))
        b.s(lambda e, o=P_sb, i=self.ps_s, k=kvalid, s=scale, m=self.col0: e.activation(o[:, 0:k], i[:, 0:k], AF.Exp, bias=m[:], scale=s))
        if masked_from is not None:
            b.v(lambda e, o=P_sb, f=masked_from: e.tensor_mul(o[:, f:f + 128], o[:, f:f + 128], self.tril[:]))
        b.v(lambda e, o=self.col1, i=P_sb, k=kvalid: e.reduce_sum(o[:], i[:, 0:k], axis=AX.X))
        b.v(lambda e, o=rs_out, i=self.col1: e.reciprocal(o[:], i[:]))

    def transpose_P(self, P_sb, Pt_sb, nchunks):
        b = self.bld
        A = self.A
        for g in range(0, nchunks, 8):
            n = min(8, nchunks - g)
            for c in range(n):
                b.tr(self.ps_tb[:, c * 128:(c + 1) * 128], P_sb[:, (g + c) * 128:(g + c + 1) * 128], self.idb[:])
            b.v(lambda e, o=Pt_sb, i=self.ps_tb, g=g, n=n: e.tensor_copy(o[:, g * 128:(g + n) * 128], i[:, 0:n * 128]))


def stream_wT(bld, cm, dst_sb, src, din, dout, dep=None):
    for ch in range(din // 128):
        bld.ld(dst_sb[:, ch * dout:(ch + 1) * dout], src[ch * 128:(ch + 1) * 128, :])



def emit_prep(bld, A):
    b = bld
    ps_f = b.psum([128, 512], F32)
    ps_b = b.psum([128, 1024], BF16)
    wld = b.sbuf([128, 1024], F32)
    wlb = b.sbuf([128, 1024], BF16)
    stg = b.sbuf([128, 1024], BF16)
    stgf = b.sbuf([128, 512], F32)
    rowf = b.sbuf([1, 4096], F32)
    spc = b.sbuf([128, 32], F32)
    tiny = b.sbuf([128, 128], F32)
    tiny2 = b.sbuf([128, 128], F32)
    tinyb = b.sbuf([128, 128], BF16)
    idb = b.sbuf([128, 128], BF16)
    idf = b.sbuf([128, 128], F32)
    onesrow = b.sbuf([1, 128], F32)
    P1T = b.sbuf([16, 512], BF16)
    P2c = b.sbuf([16, 512], BF16)
    Q1T = b.sbuf([32, 512], BF16)
    Q2c = b.sbuf([32, 512], BF16)
    b.ld(idb[:, :], A["ident_b"][:, :])
    b.ld(idf[:, :], A["ident_f"][:, :])
    b.ld(onesrow[:, :], A["ones_row_f"][:, :])
    b.ld(P1T[:, :], A["P1T"][:, :])
    b.ld(P2c[:, :], A["P2"][:, :])
    b.ld(Q1T[:, :], A["Q1T"][:, :])
    b.ld(Q2c[:, :], A["Q2"][:, :])

    def cast_nat(src, dst, R, Cc):
        r = 0
        while r < R:
            n = min(128, R - r)
            c = 0
            while c < Cc:
                w = min(1024, Cc - c)
                b.ld(wld[0:n, 0:w], src[r:r + n, c:c + w])
                b.copy(wlb[0:n, 0:w], wld[0:n, 0:w])
                b.stdma(dst[r:r + n, c:c + w], wlb[0:n, 0:w])
                c += w
            r += n

    def loadT(src, dst, R, Cc, scale_pc=None):
        for cc in range(Cc // 128):
            for rg in range(0, R // 128, 8):
                ng = min(8, R // 128 - rg)
                for j in range(ng):
                    rc = rg + j
                    b.ld(wld[:, j * 128:(j + 1) * 128], src[rc * 128:(rc + 1) * 128, cc * 128:(cc + 1) * 128])
                for j in range(ng):
                    rc = rg + j
                    if scale_pc is not None:
                        b.v(lambda e, o=wlb, i=wld, j=j, s=scale_pc, rc=rc: e.tensor_scalar_mul(
                            o[:, j * 128:(j + 1) * 128], i[:, j * 128:(j + 1) * 128], s[:, rc:rc + 1]))
                    else:
                        b.copy(wlb[:, j * 128:(j + 1) * 128], wld[:, j * 128:(j + 1) * 128])
                for j in range(ng):
                    b.tr(ps_b[:, j * 128:(j + 1) * 128], wlb[:, j * 128:(j + 1) * 128], idb[:])
                b.copy(stg[:, 0:ng * 128], ps_b[:, 0:ng * 128])
                b.stdma(dst[cc * 128:(cc + 1) * 128, rg * 128:(rg + ng) * 128], stg[:, 0:ng * 128])

    def svec_pc(src, n, dst_pc):
        b.ld(rowf[0:1, 0:n], src[0:1, 0:n])
        for rc in range(n // 128):
            b.tr(ps_f[:, 0:1].bitcast(F32), rowf[0:1, rc * 128:(rc + 1) * 128], idf[0:1, 0:1])
            b.copy(dst_pc[:, rc:rc + 1], ps_f[:, 0:1])

    def bcast_row(src, dst, n, scale=1.0):
        b.ld(rowf[0:1, 0:n], src[0:1, 0:n])
        c = 0
        while c < n:
            w = min(512, n - c)
            b.mm(ps_f[:, 0:w], onesrow[:], rowf[0:1, c:c + w])
            if scale != 1.0:
                b.v(lambda e, o=stgf, i=ps_f, w=w, s=scale: e.tensor_scalar_mul(o[:, 0:w], i[:, 0:w], s))
            else:
                b.copy(stgf[:, 0:w], ps_f[:, 0:w])
            b.stdma(dst[:, c:c + w], stgf[:, 0:w])
            c += w

    cast_nat(A["L0_attn_w_kv_down"], A["p_l0_wkvd"], 1024, 128)
    cast_nat(A["L0_attn_w_kv_up"], A["p_l0_wkvu"], 128, 2048)
    cast_nat(A["L0_attn_w_q"], A["p_l0_wq"], 1024, 1024)
    cast_nat(A["L0_attn_w_out"], A["p_l0_wout"], 1024, 1024)
    loadT(A["L0_ffn_W"], A["p_l0f_WT"], 1024, 1024)
    for cc in range(8):
        b.ld(wld[0:100, 0:128], A["L0_ffn_down_aux"][0:100, cc * 128:(cc + 1) * 128])
        b.copy(wlb[0:100, 0:128], wld[0:100, 0:128])
        b.tr(ps_b[:, 0:100], wlb[0:100, 0:128], idb[0:100, 0:100])
        b.copy(stg[:, 0:100], ps_b[:, 0:100])
        b.stdma(A["p_l0f_daT"][cc * 128:(cc + 1) * 128, 0:100], stg[:, 0:100])
    b.ld(wld[0:4, 0:100], A["L0_ffn_lora_down"][:, :])
    b.copy(wlb[0:4, 0:100], wld[0:4, 0:100])
    b.tr(ps_b[0:100, 0:4], wlb[0:4, 0:100], idb[0:4, 0:4])
    b.copy(stg[0:100, 0:4], ps_b[0:100, 0:4])
    b.stdma(A["p_l0f_ldT"][:, :], stg[0:100, 0:4])
    b.ld(wld[0:50, 0:4], A["L0_ffn_lora_up"][:, :])
    b.copy(wlb[0:50, 0:4], wld[0:50, 0:4])
    b.tr(ps_b[0:4, 0:50], wlb[0:50, 0:4], idb[0:50, 0:50])
    b.copy(stg[0:4, 0:50], ps_b[0:4, 0:50])
    b.stdma(A["p_l0f_luT"][:, :], stg[0:4, 0:50])
    for rc in range(8):
        b.ld(wld[:, 0:50], A["L0_ffn_up_aux"][rc * 128:(rc + 1) * 128, 0:50])
        b.copy(wlb[:, 0:50], wld[:, 0:50])
        b.tr(ps_b[0:50, 0:128], wlb[:, 0:50], idb[:, :])
        b.copy(stg[0:50, 0:128], ps_b[0:50, 0:128])
        b.stdma(A["p_l0f_uaT"][0:50, rc * 128:(rc + 1) * 128], stg[0:50, 0:128])
    bcast_row(A["L0_ffn_b"], A["p_l0f_b"], 1024)

    loadT(A["L1_attn_Wq"], A["p_l1_WqT"], 1024, 1024)
    loadT(A["L1_attn_Wk"], A["p_l1_WkT"], 1024, 1024)
    loadT(A["L1_attn_Wv"], A["p_l1_WvT"], 1024, 1024)
    loadT(A["L1_attn_Wo"], A["p_l1_WoT"], 1024, 1024)
    lamv = b.sbuf([1, 64], F32)
    lamv2 = b.sbuf([1, 64], F32)
    lsc = b.sbuf([1, 4], F32)
    b.ld(lamv[0:1, :], A["L1_attn_lambda_q1"][0:1, :])
    b.ld(lamv2[0:1, :], A["L1_attn_lambda_k1"][0:1, :])
    b.v(lambda e: e.tensor_mul(lamv[0:1, :], lamv[0:1, :], lamv2[0:1, :]))
    b.v(lambda e: e.reduce_sum(lsc[0:1, 0:1], lamv[0:1, :], axis=AX.X))
    b.s(lambda e: e.activation(lsc[0:1, 1:2], lsc[0:1, 0:1], AF.Exp))
    b.ld(lamv[0:1, :], A["L1_attn_lambda_q2"][0:1, :])
    b.ld(lamv2[0:1, :], A["L1_attn_lambda_k2"][0:1, :])
    b.v(lambda e: e.tensor_mul(lamv[0:1, :], lamv[0:1, :], lamv2[0:1, :]))
    b.v(lambda e: e.reduce_sum(lsc[0:1, 0:1], lamv[0:1, :], axis=AX.X))
    b.s(lambda e: e.activation(lsc[0:1, 2:3], lsc[0:1, 0:1], AF.Exp))
    b.v(lambda e: e.tensor_sub(lsc[0:1, 3:4], lsc[0:1, 1:2], lsc[0:1, 2:3]))
    b.v(lambda e: e.tensor_scalar_add(lsc[0:1, 3:4], lsc[0:1, 3:4], LAMBDA_INIT))
    b.mm(ps_f[:, 0:1], onesrow[:], lsc[0:1, 3:4])
    b.copy(stgf[:, 0:1], ps_f[:, 0:1])
    b.stdma(A["p_l1_lam"][:, :], stgf[:, 0:1])
    bcast_row(A["L1_attn_ln_w"], A["p_l1_lnw"], 128, scale=(1.0 - LAMBDA_INIT))
    bcast_row(A["L1_attn_ln_b"], A["p_l1_lnb"], 128, scale=(1.0 - LAMBDA_INIT))

    g2s = b.sbuf([16, 16], BF16)
    g1s = b.sbuf([32, 32], BF16)
    acc = b.sbuf([128, 4 * 512], F32)
    aexp = b.sbuf([128, 512], F32)
    bexp = b.sbuf([128, 512], F32)
    gp2 = b.sbuf([16, 512], BF16)
    gq2 = b.sbuf([32, 512], BF16)
    for (g1n, g2n, wtn, neg) in (("L1_ffn_G1_R", "L1_ffn_G2_R", "p_l1f_WrT", False),
                                 ("L1_ffn_G1_I", "L1_ffn_G2_I", "p_l1f_WiT", False)):
        for c in range(4):
            b.ld(wld[0:16, 0:16], A[g2n][c, :, 0, :])
            b.copy(g2s[0:16, 0:16], wld[0:16, 0:16])
            b.tr(ps_b[0:16, 0:16], g2s[0:16, 0:16], idb[0:16, 0:16])
            b.copy(tinyb[0:16, 0:16], ps_b[0:16, 0:16])
            b.mm(ps_f[0:16, 0:512], tinyb[0:16, 0:16], P2c[:, :])
            b.copy(gp2[0:16, 0:512], ps_f[0:16, 0:512])
            b.ld(wld[0:32, 0:32], A[g1n][0, :, c, :])
            b.copy(g1s[0:32, 0:32], wld[0:32, 0:32])
            b.tr(ps_b[0:32, 0:32], g1s[0:32, 0:32], idb[0:32, 0:32])
            b.copy(tinyb[0:32, 0:32], ps_b[0:32, 0:32])
            b.mm(ps_f[0:32, 0:512], tinyb[0:32, 0:32], Q2c[:, :])
            b.copy(gq2[0:32, 0:512], ps_f[0:32, 0:512])
            for rc in range(4):
                b.mm(ps_f[:, 0:512], P1T[:, rc * 128:(rc + 1) * 128], gp2[0:16, 0:512])
                b.copy(aexp[:, 0:512], ps_f[:, 0:512])
                b.mm(ps_f[:, 0:512], Q1T[:, rc * 128:(rc + 1) * 128], gq2[0:32, 0:512])
                b.copy(bexp[:, 0:512], ps_f[:, 0:512])
                if c == 0:
                    b.v(lambda e, rc=rc: e.tensor_mul(acc[:, rc * 512:(rc + 1) * 512], aexp[:, 0:512], bexp[:, 0:512]))
                else:
                    b.v(lambda e, rc=rc: e.tensor_mul(aexp[:, 0:512], aexp[:, 0:512], bexp[:, 0:512]))
                    b.v(lambda e, rc=rc: e.tensor_add(acc[:, rc * 512:(rc + 1) * 512], acc[:, rc * 512:(rc + 1) * 512], aexp[:, 0:512]))
        for cc in range(4):
            for rc in range(4):
                b.copy(wlb[:, 0:128], acc[:, rc * 512 + cc * 128: rc * 512 + (cc + 1) * 128])
                b.tr(ps_b[:, rc * 128:(rc + 1) * 128], wlb[:, 0:128], idb[:])
            b.copy(stg[:, 0:512], ps_b[:, 0:512])
            b.stdma(A[wtn][cc * 128:(cc + 1) * 128, :], stg[:, 0:512])
            if wtn == "p_l1f_WiT":
                b.v(lambda e: e.tensor_scalar_mul(stg[:, 0:512], stg[:, 0:512], -1.0))
                b.stdma(A["p_l1f_WiTn"][cc * 128:(cc + 1) * 128, :], stg[:, 0:512])
    bcast_row(A["L1_ffn_bias_real"], A["p_l1f_br"], 512)
    bcast_row(A["L1_ffn_bias_imag"], A["p_l1f_bi"], 512)

    for h in range(16):
        b.ld(wld[0:64, 0:64], A["L2_attn_Wq"][h, :, :])
        b.copy(wlb[0:64, 0:64], wld[0:64, 0:64])
        b.tr(ps_b[0:64, 0:64], wlb[0:64, 0:64], idb[0:64, 0:64])
        b.copy(stg[0:64, 0:64], ps_b[0:64, 0:64])
        b.stdma(A["p_l2_WqT"][h * 64:(h + 1) * 64, :], stg[0:64, 0:64])
    loadT(A["L2_attn_Wp"], A["p_l2_WpT"], 1024, 1024)
    bcast_row(A["L2_attn_bp"], A["p_l2_bp"], 1024)
    svec_pc(A["L2_ffn_w1_s"], 4096, spc)
    loadT(A["L2_ffn_w1_w"], A["p_l2f_w1T"], 4096, 1024, scale_pc=spc)
    svec_pc(A["L2_ffn_w2_s"], 4096, spc)
    loadT(A["L2_ffn_w2_w"], A["p_l2f_w2T"], 4096, 1024, scale_pc=spc)
    svec_pc(A["L2_ffn_h1_s"], 4096, spc)
    loadT(A["L2_ffn_h1_w"], A["p_l2f_h1T"], 4096, 4096, scale_pc=spc)
    svec_pc(A["L2_ffn_h2_s"], 4096, spc)
    loadT(A["L2_ffn_h2_w"], A["p_l2f_h2T"], 4096, 4096, scale_pc=spc)
    svec_pc(A["L2_ffn_w3_s"], 1024, spc)
    loadT(A["L2_ffn_w3_w"], A["p_l2f_w3T"], 1024, 4096, scale_pc=spc)

    cast_nat(A["L3_attn_w_cq"], A["p_l3_wcq"], 1024, 96)
    cast_nat(A["L3_attn_w_dq_nope"], A["p_l3_wdqn"], 96, 512)
    cast_nat(A["L3_attn_w_dq_rope"], A["p_l3_wdqr"], 96, 1024)
    cast_nat(A["L3_attn_w_ckv"], A["p_l3_wckv"], 1024, 32)
    cast_nat(A["L3_attn_w_dk_nope"], A["p_l3_wdkn"], 32, 512)
    cast_nat(A["L3_attn_w_dv"], A["p_l3_wdv"], 32, 512)
    cast_nat(A["L3_attn_w_krope"], A["p_l3_wkrope"], 1024, 64)
    cast_nat(A["L3_attn_w_selk"], A["p_l3_wselk"], 1024, 1536)
    cast_nat(A["L3_attn_w_selv"], A["p_l3_wselv"], 1024, 512)
    cast_nat(A["L3_attn_w_wink"], A["p_l3_wwink"], 1024, 1536)
    cast_nat(A["L3_attn_w_winv"], A["p_l3_wwinv"], 1024, 512)
    cast_nat(A["L3_attn_w_gate"], A["p_l3_wgate"], 1024, 3)
    cast_nat(A["L3_attn_w_proj"], A["p_l3_wproj"], 512, 1024)
    bcast_row(A["L3_attn_g_qnorm"], A["p_l3_gq"], 96)
    bcast_row(A["L3_attn_g_kvnorm"], A["p_l3_gkv"], 32)
    b.ld(wld[:, 0:8], A["L3_attn_w_imp"][:, :].rearrange("(c p) one -> p (c one)", p=128), slow=True)
    b.copy(wlb[:, 0:8], wld[:, 0:8])
    b.stdma(A["p_l3_wimp_hi"][:, :], wlb[:, 0:8])
    b.copy(tiny[:, 0:8], wlb[:, 0:8])
    b.v(lambda e: e.tensor_sub(tiny2[:, 0:8], wld[:, 0:8], tiny[:, 0:8]))
    b.copy(wlb[:, 0:8], tiny2[:, 0:8])
    b.stdma(A["p_l3_wimp_lo"][:, :], wlb[:, 0:8])
    for cc in range(8):
        for rc in range(8):
            b.ld(wld[:, 0:128], A["L3_ffn_weight"][rc * 128:(rc + 1) * 128, cc * 128:(cc + 1) * 128])
            b.s(lambda e: e.sign(wlb[:, 0:128], wld[:, 0:128]))
            b.tr(ps_b[:, rc * 128:(rc + 1) * 128], wlb[:, 0:128], idb[:])
        b.copy(stg[:, 0:1024], ps_b[:, 0:1024])
        b.stdma(A["p_l3f_sWT"][cc * 128:(cc + 1) * 128, :], stg[:, 0:1024])


PREP_OUT = {
    "p_l0_wkvd": ((1024, 128), BF16), "p_l0_wkvu": ((128, 2048), BF16),
    "p_l0_wq": ((1024, 1024), BF16), "p_l0_wout": ((1024, 1024), BF16),
    "p_l0f_WT": ((1024, 1024), BF16), "p_l0f_daT": ((1024, 100), BF16),
    "p_l0f_ldT": ((100, 4), BF16), "p_l0f_luT": ((4, 50), BF16),
    "p_l0f_uaT": ((50, 1024), BF16), "p_l0f_b": ((128, 1024), F32),
    "p_l1_WqT": ((1024, 1024), BF16), "p_l1_WkT": ((1024, 1024), BF16),
    "p_l1_WvT": ((1024, 1024), BF16), "p_l1_WoT": ((1024, 1024), BF16),
    "p_l1_lam": ((128, 1), F32), "p_l1_lnw": ((128, 128), F32), "p_l1_lnb": ((128, 128), F32),
    "p_l1f_WrT": ((512, 512), BF16), "p_l1f_WiT": ((512, 512), BF16),
    "p_l1f_WiTn": ((512, 512), BF16), "p_l1f_br": ((128, 512), F32), "p_l1f_bi": ((128, 512), F32),
    "p_l2_WqT": ((1024, 64), BF16), "p_l2_WpT": ((1024, 1024), BF16), "p_l2_bp": ((128, 1024), F32),
    "p_l2f_w1T": ((1024, 4096), BF16), "p_l2f_w2T": ((1024, 4096), BF16),
    "p_l2f_h1T": ((4096, 4096), BF16), "p_l2f_h2T": ((4096, 4096), BF16),
    "p_l2f_w3T": ((4096, 1024), BF16),
    "p_l3_wcq": ((1024, 96), BF16), "p_l3_gq": ((128, 96), F32),
    "p_l3_wdqn": ((96, 512), BF16), "p_l3_wdqr": ((96, 1024), BF16),
    "p_l3_wckv": ((1024, 32), BF16), "p_l3_gkv": ((128, 32), F32),
    "p_l3_wdkn": ((32, 512), BF16), "p_l3_wdv": ((32, 512), BF16),
    "p_l3_wkrope": ((1024, 64), BF16),
    "p_l3_wimp_hi": ((128, 8), BF16), "p_l3_wimp_lo": ((128, 8), BF16),
    "p_l3_wselk": ((1024, 1536), BF16), "p_l3_wselv": ((1024, 512), BF16),
    "p_l3_wwink": ((1024, 1536), BF16), "p_l3_wwinv": ((1024, 512), BF16),
    "p_l3_wgate": ((1024, 3), BF16), "p_l3_wproj": ((512, 1024), BF16),
    "p_l3f_sWT": ((1024, 1024), BF16),
}

PREP_W_IN = [
    "L0_attn_w_kv_down", "L0_attn_w_kv_up", "L0_attn_w_q", "L0_attn_w_out",
    "L0_ffn_W", "L0_ffn_b", "L0_ffn_lora_down", "L0_ffn_lora_up", "L0_ffn_down_aux", "L0_ffn_up_aux",
    "L1_attn_Wq", "L1_attn_Wk", "L1_attn_Wv", "L1_attn_Wo",
    "L1_attn_lambda_q1", "L1_attn_lambda_k1", "L1_attn_lambda_q2", "L1_attn_lambda_k2",
    "L1_attn_ln_w", "L1_attn_ln_b",
    "L1_ffn_G1_R", "L1_ffn_G2_R", "L1_ffn_G1_I", "L1_ffn_G2_I", "L1_ffn_bias_real", "L1_ffn_bias_imag",
    "L2_attn_Wq", "L2_attn_Wp", "L2_attn_bp",
    "L2_ffn_w1_w", "L2_ffn_w1_s", "L2_ffn_w2_w", "L2_ffn_w2_s",
    "L2_ffn_h1_w", "L2_ffn_h1_s", "L2_ffn_h2_w", "L2_ffn_h2_s", "L2_ffn_w3_w", "L2_ffn_w3_s",
    "L3_attn_w_cq", "L3_attn_g_qnorm", "L3_attn_w_dq_nope", "L3_attn_w_dq_rope",
    "L3_attn_w_ckv", "L3_attn_g_kvnorm", "L3_attn_w_dk_nope", "L3_attn_w_dv",
    "L3_attn_w_krope", "L3_attn_w_imp", "L3_attn_w_selk", "L3_attn_w_selv",
    "L3_attn_w_wink", "L3_attn_w_winv", "L3_attn_w_gate", "L3_attn_w_proj",
    "L3_ffn_weight",
]

PREP_IN_SHAPES = {
    "L0_attn_w_kv_down": (1024, 128), "L0_attn_w_kv_up": (128, 2048),
    "L0_attn_w_q": (1024, 1024), "L0_attn_w_out": (1024, 1024),
    "L0_ffn_W": (1024, 1024), "L0_ffn_b": (1, 1024),
    "L0_ffn_lora_down": (4, 100), "L0_ffn_lora_up": (50, 4),
    "L0_ffn_down_aux": (100, 1024), "L0_ffn_up_aux": (1024, 50),
    "L1_attn_Wq": (1024, 1024), "L1_attn_Wk": (1024, 1024),
    "L1_attn_Wv": (1024, 1024), "L1_attn_Wo": (1024, 1024),
    "L1_attn_lambda_q1": (1, 64), "L1_attn_lambda_k1": (1, 64),
    "L1_attn_lambda_q2": (1, 64), "L1_attn_lambda_k2": (1, 64),
    "L1_attn_ln_w": (1, 128), "L1_attn_ln_b": (1, 128),
    "L1_ffn_G1_R": (1, 32, 4, 32), "L1_ffn_G2_R": (4, 16, 1, 16),
    "L1_ffn_G1_I": (1, 32, 4, 32), "L1_ffn_G2_I": (4, 16, 1, 16),
    "L1_ffn_bias_real": (1, 512), "L1_ffn_bias_imag": (1, 512),
    "L2_attn_Wq": (16, 64, 64), "L2_attn_Wp": (1024, 1024), "L2_attn_bp": (1, 1024),
    "L2_ffn_w1_w": (4096, 1024), "L2_ffn_w1_s": (1, 4096),
    "L2_ffn_w2_w": (4096, 1024), "L2_ffn_w2_s": (1, 4096),
    "L2_ffn_h1_w": (4096, 4096), "L2_ffn_h1_s": (1, 4096),
    "L2_ffn_h2_w": (4096, 4096), "L2_ffn_h2_s": (1, 4096),
    "L2_ffn_w3_w": (1024, 4096), "L2_ffn_w3_s": (1, 1024),
    "L3_attn_w_cq": (1024, 96), "L3_attn_g_qnorm": (1, 96),
    "L3_attn_w_dq_nope": (96, 512), "L3_attn_w_dq_rope": (96, 1024),
    "L3_attn_w_ckv": (1024, 32), "L3_attn_g_kvnorm": (1, 32),
    "L3_attn_w_dk_nope": (32, 512), "L3_attn_w_dv": (32, 512),
    "L3_attn_w_krope": (1024, 64), "L3_attn_w_imp": (1024, 1),
    "L3_attn_w_selk": (1024, 1536), "L3_attn_w_selv": (1024, 512),
    "L3_attn_w_wink": (1024, 1536), "L3_attn_w_winv": (1024, 512),
    "L3_attn_w_gate": (1024, 3), "L3_attn_w_proj": (512, 1024),
    "L3_ffn_weight": (1024, 1024),
}

def _kron_consts():
    P1 = np.zeros((512, 16), dtype=np.float32)
    for be in range(512):
        P1[be, be % 16] = 1.0
    P2 = np.zeros((16, 512), dtype=np.float32)
    for dg in range(512):
        P2[dg % 16, dg] = 1.0
    Q1 = np.zeros((512, 32), dtype=np.float32)
    for be in range(512):
        Q1[be, be // 16] = 1.0
    Q2 = np.zeros((32, 512), dtype=np.float32)
    for dg in range(512):
        Q2[dg // 16, dg] = 1.0
    return {"P1T": P1.T.astype(BF).copy(), "P2": P2.astype(BF), "Q1T": Q1.T.astype(BF).copy(), "Q2": Q2.astype(BF)}


CONSTS.update(_kron_consts())
CONST_IO.update({"P1T": ((16, 512), BF16), "P2": ((16, 512), BF16),
                 "Q1T": ((32, 512), BF16), "Q2": ((32, 512), BF16)})


def prep_io():
    io = {}
    for n in PREP_W_IN:
        io[n] = (PREP_IN_SHAPES[n], F32, "ExternalInput")
    for n, (shape, dt) in PREP_OUT.items():
        io[n] = (shape, dt, "ExternalOutput")
    io.update(io_inputs(["ident_b", "ident_f", "ones_row_f", "P1T", "P2", "Q1T", "Q2"]))
    return io



def emit_l0attn(bld, A):
    b = bld
    cm = Common(bld, A)
    Xnt = b.sbuf([128, 8 * T], BF16)
    Qt = b.sbuf([128, 8 * T], BF16)
    Kt = b.sbuf([128, 8 * T], BF16)
    V = b.sbuf([128, NT * 1024], BF16)
    Lat = b.sbuf([128, T], BF16)
    wkvd = b.sbuf([128, 8 * 128], BF16)
    wkvu = b.sbuf([128, 2048], BF16)
    wsc = b.sbuf([128, 8 * 128], BF16)
    wout = b.sbuf([128, 8 * 1024], BF16)
    P_sb = b.sbuf([128, 2048], BF16)
    Pt_sb = b.sbuf([128, 2048], BF16)
    Yb = b.sbuf([128, 1024], BF16)
    Yt = b.sbuf([128, 1024], BF16)
    rs = b.sbuf([128, 1], F32)
    xin, y = A["xin"], A["y"]

    for ti in range(NT):
        cm.ln_tile(xin, ti)
        cm.tile_to_fmajor(cm.xh_b, Xnt, ti)
    stream_wT(b, cm, wkvd, A["p_l0_wkvd"], 1024, 128)
    b.ld(wkvu[:, :], A["p_l0_wkvu"][:, :])
    stream_wT(b, cm, wout, A["p_l0_wout"], 1024, 1024)
    for tc in range(T // 512):
        for k in range(8):
            b.mm(cm.ps0[:, :], wkvd[:, k * 128:(k + 1) * 128],
                 Xnt[:, k * T + tc * 512: k * T + tc * 512 + 512], first=(k == 0), last=(k == 7))
        b.copy(Lat[:, tc * 512:(tc + 1) * 512], cm.ps0[:, :])
    for m in range(8):
        for tc in range(T // 512):
            b.mm(cm.ps0[:, :], wkvu[:, m * 128:(m + 1) * 128], Lat[:, tc * 512:(tc + 1) * 512])
            b.v(lambda e, m=m, tc=tc: e.tensor_scalar_mul(
                Kt[:, m * T + tc * 512: m * T + (tc) * 512 + 512], cm.ps0[:, :], 0.125))
    for ti in range(NT):
        for n in range(2):
            b.mm(cm.ps0[:, :], Lat[:, ti * 128:(ti + 1) * 128], wkvu[:, 1024 + n * 512:1024 + (n + 1) * 512])
            b.copy(V[:, ti * 1024 + n * 512: ti * 1024 + (n + 1) * 512], cm.ps0[:, :])
    for m in range(8):
        for k in range(8):
            b.ld(wsc[:, k * 128:(k + 1) * 128], A["p_l0_wq"][k * 128:(k + 1) * 128, m * 128:(m + 1) * 128])
        for tc in range(T // 512):
            for k in range(8):
                b.mm(cm.ps0[:, :], wsc[:, k * 128:(k + 1) * 128],
                     Xnt[:, k * T + tc * 512: k * T + tc * 512 + 512], first=(k == 0), last=(k == 7))
            b.copy(Qt[:, m * T + tc * 512: m * T + (tc + 1) * 512], cm.ps0[:, :])
    for qi in range(NT):
        kv = (qi + 1) * 128
        for h in range(16):
            hc, off = h // 2, (h % 2) * 64
            for sc in range((kv + 511) // 512):
                w = min(512, kv - sc * 512)
                b.mm(cm.ps_s[:, sc * 512: sc * 512 + w],
                     Qt[off:off + 64, hc * T + qi * 128: hc * T + qi * 128 + 128],
                     Kt[off:off + 64, hc * T + sc * 512: hc * T + sc * 512 + w])
            cm.softmax(kv, P_sb, masked_from=qi * 128, rs_out=rs)
            cm.transpose_P(P_sb, Pt_sb, qi + 1)
            for kc in range(qi + 1):
                b.mm(cm.ps0[:, 0:64], Pt_sb[:, kc * 128:(kc + 1) * 128],
                     V[:, kc * 1024 + h * 64: kc * 1024 + h * 64 + 64], first=(kc == 0), last=(kc == qi))
            b.v(lambda e, h=h: e.tensor_scalar_mul(Yb[:, h * 64:(h + 1) * 64], cm.ps0[:, 0:64], rs[:]))
        for c in range(8):
            b.tr(cm.ps_tb[:, c * 128:(c + 1) * 128], Yb[:, c * 128:(c + 1) * 128], cm.idb[:])
        b.copy(Yt[:, :], cm.ps_tb[:, :])
        b.ld(cm.xt[:], xin[qi * 128:(qi + 1) * 128, :])
        for n in range(2):
            for k in range(8):
                b.mm(cm.ps1[:, :], Yt[:, k * 128:(k + 1) * 128],
                     wout[:, k * 1024 + n * 512: k * 1024 + n * 512 + 512], first=(k == 0), last=(k == 7))
            b.v(lambda e, n=n: e.scalar_tensor_tensor(
                cm.ysb[:, n * 512:(n + 1) * 512], cm.ps1[:, :], 1.0, cm.xt[:, n * 512:(n + 1) * 512],
                ALU.mult, ALU.add))
        b.stdma(y[qi * 128:(qi + 1) * 128, :], cm.ysb[:, :])



def emit_l0ffn(bld, A):
    b = bld
    cm = Common(bld, A)
    Xnt_t = b.sbuf([128, 1024], BF16)
    WT = b.sbuf([128, 8 * 1024], BF16)
    daT = b.sbuf([128, 8 * 100], BF16)
    ldT = b.sbuf([100, 4], BF16)
    luT = b.sbuf([4, 50], BF16)
    uaT = b.sbuf([50, 1024], BF16)
    bias = b.sbuf([128, 1024], F32)
    aT = b.sbuf([100, 128], BF16)
    bT = b.sbuf([4, 128], BF16)
    cT = b.sbuf([50, 128], BF16)
    xin, y = A["xin"], A["y"]

    stream_wT(b, cm, WT, A["p_l0f_WT"], 1024, 1024)
    for ch in range(8):
        b.ld(daT[:, ch * 100:(ch + 1) * 100], A["p_l0f_daT"][ch * 128:(ch + 1) * 128, :])
    b.ld(ldT[:, :], A["p_l0f_ldT"][:, :])
    b.ld(luT[:, :], A["p_l0f_luT"][:, :])
    b.ld(uaT[:, :], A["p_l0f_uaT"][:, :])
    b.ld(bias[:, :], A["p_l0f_b"][:, :])
    for ti in range(NT):
        cm.ln_tile(xin, ti)
        cm.tile_to_fmajor(cm.xh_b, Xnt_t, 0, width=128)
        for k in range(8):
            b.mm(cm.ps0[0:100, 0:128], daT[:, k * 100:(k + 1) * 100],
                 Xnt_t[:, k * 128:(k + 1) * 128], first=(k == 0), last=(k == 7))
        b.copy(aT[:, :], cm.ps0[0:100, 0:128])
        b.mm(cm.ps1[0:4, 0:128], ldT[:, :], aT[:, :])
        b.copy(bT[:, :], cm.ps1[0:4, 0:128])
        b.mm(cm.ps2[0:50, 0:128], luT[:, :], bT[:, :])
        b.copy(cT[:, :], cm.ps2[0:50, 0:128])
        for n in range(2):
            for k in range(8):
                b.mm(cm.ps0[:, :], Xnt_t[:, k * 128:(k + 1) * 128],
                     WT[:, k * 1024 + n * 512: k * 1024 + n * 512 + 512], first=(k == 0), last=False)
            b.mm(cm.ps0[:, :], cT[:, :], uaT[:, n * 512:(n + 1) * 512], first=False, last=True)
            b.v(lambda e, n=n: e.scalar_tensor_tensor(
                cm.ysb[:, n * 512:(n + 1) * 512], cm.ps0[:, :], 1.0, bias[:, n * 512:(n + 1) * 512],
                ALU.mult, ALU.add))
            b.v(lambda e, n=n: e.tensor_add(
                cm.ysb[:, n * 512:(n + 1) * 512], cm.ysb[:, n * 512:(n + 1) * 512],
                cm.xt[:, n * 512:(n + 1) * 512]))
        b.stdma(y[ti * 128:(ti + 1) * 128, :], cm.ysb[:, :])



def emit_l1attn(bld, A):
    b = bld
    cm = Common(bld, A)
    Xnt = b.sbuf([128, 8 * T], BF16)
    Qt = b.sbuf([128, 8 * T], BF16)
    Kt = b.sbuf([128, 8 * T], BF16)
    V = b.sbuf([128, NT * 1024], BF16)
    wqt = b.sbuf([128, 8 * 1024], BF16)
    P_sb = b.sbuf([128, 2048], BF16)
    Pt_sb = b.sbuf([128, 2048], BF16)
    O = b.sbuf([128, 1024], F32)
    Ob = b.sbuf([128, 1024], BF16)
    Yt = b.sbuf([128, 1024], BF16)
    lnw8 = b.sbuf([128, 1024], F32)
    lnb8 = b.sbuf([128, 1024], F32)
    lam = b.sbuf([128, 1], F32)
    neglam = b.sbuf([128, 1], F32)
    rs1 = b.sbuf([128, 1], F32)
    rs2 = b.sbuf([128, 1], F32)
    rs2n = b.sbuf([128, 1], F32)
    cos_t = b.sbuf([128, 512], F32)
    sin_t = b.sbuf([128, 512], F32)
    qf = cm.xt
    rt = cm.xc
    WoT = wqt
    xin, y = A["xin"], A["y"]

    for ti in range(NT):
        cm.ln_tile(xin, ti)
        cm.tile_to_fmajor(cm.xh_b, Xnt, ti)
    b.ld(lam[:, :], A["p_l1_lam"][:, :])
    b.v(lambda e: e.tensor_scalar_mul(neglam[:, :], lam[:, :], -1.0))
    b.ld(lnw8[:, 0:128], A["p_l1_lnw"][:, :])
    b.ld(lnb8[:, 0:128], A["p_l1_lnb"][:, :])
    for h in range(1, 8):
        b.copy(lnw8[:, h * 128:(h + 1) * 128], lnw8[:, 0:128])
        b.copy(lnb8[:, h * 128:(h + 1) * 128], lnb8[:, 0:128])

    def qk_pass(wname, dst, scale):
        stream_wT(b, cm, wqt, A[wname], 1024, 1024)
        for ti in range(NT):
            b.ld(cos_t[:, :], A["cos1"][ti * 128:(ti + 1) * 128, :])
            b.ld(sin_t[:, :], A["sin1"][ti * 128:(ti + 1) * 128, :])
            for n in range(2):
                for k in range(8):
                    b.mm(cm.ps0[:, :], Xnt[:, k * T + ti * 128: k * T + ti * 128 + 128],
                         wqt[:, k * 1024 + n * 512: k * 1024 + n * 512 + 512],
                         first=(k == 0), last=(k == 7))
                b.v(lambda e, n=n, s=scale: e.tensor_scalar_mul(qf[:, n * 512:(n + 1) * 512], cm.ps0[:, :], s))
            ev = qf[:, 0:1024:2]
            ov = qf[:, 1:1024:2]
            b.v(lambda e: e.tensor_mul(rt[:, 0:512], ev, cos_t[:, :]))
            b.v(lambda e: e.tensor_mul(rt[:, 512:1024], ov, cos_t[:, :]))
            b.v(lambda e: e.tensor_mul(cm.sq[:, 0:512], ev, sin_t[:, :]))
            b.v(lambda e: e.tensor_mul(cm.sq[:, 512:1024], ov, sin_t[:, :]))
            b.v(lambda e: e.tensor_sub(ev, rt[:, 0:512], cm.sq[:, 512:1024]))
            b.v(lambda e: e.tensor_add(ov, rt[:, 512:1024], cm.sq[:, 0:512]))
            b.copy(cm.xh_b[:, :], qf[:, :])
            cm.tile_to_fmajor(cm.xh_b, dst, ti)

    qk_pass("p_l1_WqT", Qt, 64.0 ** (-0.25))
    qk_pass("p_l1_WkT", Kt, 1.0)
    stream_wT(b, cm, wqt, A["p_l1_WvT"], 1024, 1024)
    for ti in range(NT):
        for n in range(2):
            for k in range(8):
                b.mm(cm.ps0[:, :], Xnt[:, k * T + ti * 128: k * T + ti * 128 + 128],
                     wqt[:, k * 1024 + n * 512: k * 1024 + n * 512 + 512], first=(k == 0), last=(k == 7))
            b.copy(V[:, ti * 1024 + n * 512: ti * 1024 + (n + 1) * 512], cm.ps0[:, :])
    stream_wT(b, cm, WoT, A["p_l1_WoT"], 1024, 1024)

    for qi in range(NT):
        for h in range(8):
            for var in range(2):
                off = var * 64
                for sc in range(4):
                    b.mm(cm.ps_s[:, sc * 512:(sc + 1) * 512],
                         Qt[off:off + 64, h * T + qi * 128: h * T + qi * 128 + 128],
                         Kt[off:off + 64, h * T + sc * 512: h * T + (sc + 1) * 512])
                cm.softmax(T, P_sb, rs_out=(rs1 if var == 0 else rs2))
                cm.transpose_P(P_sb, Pt_sb, NT)
                ps = cm.ps0 if var == 0 else cm.ps1
                for kc in range(NT):
                    b.mm(ps[:, 0:128], Pt_sb[:, kc * 128:(kc + 1) * 128],
                         V[:, kc * 1024 + h * 128: kc * 1024 + (h + 1) * 128],
                         first=(kc == 0), last=(kc == NT - 1))
            b.v(lambda e, h=h: e.tensor_scalar_mul(O[:, h * 128:(h + 1) * 128], cm.ps0[:, 0:128], rs1[:]))
            b.v(lambda e: e.tensor_mul(rs2n[:, :], rs2[:, :], neglam[:, :]))
            b.v(lambda e, h=h: e.scalar_tensor_tensor(
                O[:, h * 128:(h + 1) * 128], cm.ps1[:, 0:128], rs2n[:], O[:, h * 128:(h + 1) * 128],
                ALU.mult, ALU.add))
        b.v(lambda e: e.tensor_mul(cm.sq[:, :], O[:, :], O[:, :]))
        for h in range(8):
            b.v(lambda e, h=h: e.reduce_sum(cm.col0[:], cm.sq[:, h * 128:(h + 1) * 128], axis=AX.X))
            b.s(lambda e: e.activation(cm.col1[:], cm.col0[:], AF.Sqrt, bias=cm.c_eps8[:], scale=1.0 / 128.0))
            b.v(lambda e: e.reciprocal(cm.col2[:], cm.col1[:]))
            b.v(lambda e, h=h: e.tensor_scalar_mul(O[:, h * 128:(h + 1) * 128], O[:, h * 128:(h + 1) * 128], cm.col2[:]))
        b.v(lambda e: e.tensor_mul(O[:, :], O[:, :], lnw8[:, :]))
        b.v(lambda e: e.tensor_add(O[:, :], O[:, :], lnb8[:, :]))
        b.copy(Ob[:, :], O[:, :])
        for c in range(8):
            b.tr(cm.ps_tb[:, c * 128:(c + 1) * 128], Ob[:, c * 128:(c + 1) * 128], cm.idb[:])
        b.copy(Yt[:, :], cm.ps_tb[:, :])
        b.ld(cm.xt[:], xin[qi * 128:(qi + 1) * 128, :])
        for n in range(2):
            for k in range(8):
                b.mm(cm.ps2[:, :], Yt[:, k * 128:(k + 1) * 128],
                     WoT[:, k * 1024 + n * 512: k * 1024 + n * 512 + 512], first=(k == 0), last=(k == 7))
            b.v(lambda e, n=n: e.scalar_tensor_tensor(
                cm.ysb[:, n * 512:(n + 1) * 512], cm.ps2[:, :], 1.0, cm.xt[:, n * 512:(n + 1) * 512],
                ALU.mult, ALU.add))
        b.stdma(y[qi * 128:(qi + 1) * 128, :], cm.ysb[:, :])



def emit_l1ffn(bld, A):
    b = bld
    cm = Common(bld, A)
    Xnt_t = b.sbuf([128, 1024], BF16)
    WrT = b.sbuf([128, 4 * 512], BF16)
    WiT = b.sbuf([128, 4 * 512], BF16)
    WiTn = b.sbuf([128, 4 * 512], BF16)
    br = b.sbuf([128, 512], F32)
    bi = b.sbuf([128, 512], F32)
    xin, y = A["xin"], A["y"]
    for ch in range(4):
        b.ld(WrT[:, ch * 512:(ch + 1) * 512], A["p_l1f_WrT"][ch * 128:(ch + 1) * 128, :])
        b.ld(WiT[:, ch * 512:(ch + 1) * 512], A["p_l1f_WiT"][ch * 128:(ch + 1) * 128, :])
        b.ld(WiTn[:, ch * 512:(ch + 1) * 512], A["p_l1f_WiTn"][ch * 128:(ch + 1) * 128, :])
    b.ld(br[:, :], A["p_l1f_br"][:, :])
    b.ld(bi[:, :], A["p_l1f_bi"][:, :])
    for ti in range(NT):
        cm.ln_tile(xin, ti)
        cm.tile_to_fmajor(cm.xh_b, Xnt_t, 0, width=128)
        for k in range(8):
            rhs = WrT[:, k * 512:(k + 1) * 512] if k < 4 else WiTn[:, (k - 4) * 512:(k - 3) * 512]
            b.mm(cm.ps0[:, :], Xnt_t[:, k * 128:(k + 1) * 128], rhs, first=(k == 0), last=(k == 7))
        for k in range(8):
            rhs = WiT[:, k * 512:(k + 1) * 512] if k < 4 else WrT[:, (k - 4) * 512:(k - 3) * 512]
            b.mm(cm.ps1[:, :], Xnt_t[:, k * 128:(k + 1) * 128], rhs, first=(k == 0), last=(k == 7))
        b.v(lambda e: e.scalar_tensor_tensor(cm.ysb[:, 0:512], cm.ps0[:, :], 1.0, br[:, :], ALU.mult, ALU.add))
        b.v(lambda e: e.tensor_add(cm.ysb[:, 0:512], cm.ysb[:, 0:512], cm.xt[:, 0:512]))
        b.v(lambda e: e.scalar_tensor_tensor(cm.ysb[:, 512:1024], cm.ps1[:, :], 1.0, bi[:, :], ALU.mult, ALU.add))
        b.v(lambda e: e.tensor_add(cm.ysb[:, 512:1024], cm.ysb[:, 512:1024], cm.xt[:, 512:1024]))
        b.stdma(y[ti * 128:(ti + 1) * 128, :], cm.ysb[:, :])



def emit_l2attn(bld, A):
    b = bld
    cm = Common(bld, A)
    Xnt = b.sbuf([128, 8 * T], BF16)
    Qt = b.sbuf([128, 8 * T], BF16)
    Qtm = b.sbuf([128, NT * 1024], BF16)
    WqT = b.sbuf([128, 8 * 64], BF16)
    WpT = b.sbuf([128, 8 * 1024], BF16)
    bp = b.sbuf([128, 1024], F32)
    P_sb = b.sbuf([128, 2048], BF16)
    Pt_sb = b.sbuf([128, 2048], BF16)
    Yb = b.sbuf([128, 1024], BF16)
    Yt = b.sbuf([128, 1024], BF16)
    rs = b.sbuf([128, 1], F32)
    xin, y = A["xin"], A["y"]
    SQ8 = math.sqrt(8.0)

    for ti in range(NT):
        cm.ln_tile(xin, ti)
        cm.tile_to_fmajor(cm.xh_b, Xnt, ti)
    stream_wT(b, cm, WqT, A["p_l2_WqT"], 1024, 64)
    stream_wT(b, cm, WpT, A["p_l2_WpT"], 1024, 1024)
    b.ld(bp[:, :], A["p_l2_bp"][:, :])
    for h in range(16):
        po, cc = (h % 2) * 64, h // 2
        wq_h = WqT[po:po + 64, cc * 64:(cc + 1) * 64]
        for tc in range(T // 512):
            b.mm(cm.ps0[0:64, :], wq_h, Xnt[po:po + 64, cc * T + tc * 512: cc * T + (tc + 1) * 512])
            b.v(lambda e, h=h, tc=tc, po=po, cc=cc: e.tensor_scalar_mul(
                Qt[po:po + 64, cc * T + tc * 512: cc * T + (tc + 1) * 512], cm.ps0[0:64, :], SQ8))
        for ti in range(NT):
            b.mm(cm.ps1[:, 0:64], Xnt[po:po + 64, cc * T + ti * 128: cc * T + ti * 128 + 128], wq_h)
            b.copy(Qtm[:, ti * 1024 + h * 64: ti * 1024 + h * 64 + 64], cm.ps1[:, 0:64])
    for qi in range(NT):
        for h in range(16):
            po, cc = (h % 2) * 64, h // 2
            for sc in range(4):
                b.mm(cm.ps_s[:, sc * 512:(sc + 1) * 512],
                     Qt[po:po + 64, cc * T + qi * 128: cc * T + qi * 128 + 128],
                     Qt[po:po + 64, cc * T + sc * 512: cc * T + (sc + 1) * 512])
            cm.softmax(T, P_sb, rs_out=rs)
            cm.transpose_P(P_sb, Pt_sb, NT)
            for kc in range(NT):
                b.mm(cm.ps0[:, 0:64], Pt_sb[:, kc * 128:(kc + 1) * 128],
                     Qtm[:, kc * 1024 + h * 64: kc * 1024 + h * 64 + 64], first=(kc == 0), last=(kc == NT - 1))
            b.v(lambda e, h=h: e.tensor_scalar_mul(Yb[:, h * 64:(h + 1) * 64], cm.ps0[:, 0:64], rs[:]))
        for c in range(8):
            b.tr(cm.ps_tb[:, c * 128:(c + 1) * 128], Yb[:, c * 128:(c + 1) * 128], cm.idb[:])
        b.copy(Yt[:, :], cm.ps_tb[:, :])
        b.ld(cm.xt[:], xin[qi * 128:(qi + 1) * 128, :])
        for n in range(2):
            for k in range(8):
                b.mm(cm.ps1[:, :], Yt[:, k * 128:(k + 1) * 128],
                     WpT[:, k * 1024 + n * 512: k * 1024 + n * 512 + 512], first=(k == 0), last=(k == 7))
            b.v(lambda e, n=n: e.scalar_tensor_tensor(
                cm.ysb[:, n * 512:(n + 1) * 512], cm.ps1[:, :], 1.0, bp[:, n * 512:(n + 1) * 512],
                ALU.mult, ALU.add))
            b.v(lambda e, n=n: e.tensor_add(
                cm.ysb[:, n * 512:(n + 1) * 512], cm.ysb[:, n * 512:(n + 1) * 512],
                cm.xt[:, n * 512:(n + 1) * 512]))
        b.stdma(y[qi * 128:(qi + 1) * 128, :], cm.ysb[:, :])



def emit_l2ffn(bld, A):
    b = bld
    cm = Common(bld, A)
    QT = 512
    NQ = QT // 128
    Xnt_q = b.sbuf([128, 8 * QT], BF16)
    Ht = b.sbuf([128, 32 * QT], BF16)
    H1t = b.sbuf([128, 32 * QT], BF16)
    wbuf = b.sbuf([128, 32 * 512], BF16)
    w1b = b.sbuf([128, 8 * 512], BF16)
    w2b = b.sbuf([128, 8 * 512], BF16)
    stg1 = b.sbuf([128, 512], F32)
    stgb = b.sbuf([128, 512], BF16)
    ysb2 = b.sbuf([128, 4 * 1024], F32)
    xin, y = A["xin"], A["y"]

    for q in range(4):
        tok0 = q * QT
        for tj in range(NQ):
            cm.ln_tile(xin, q * NQ + tj)
            cm.tile_to_fmajor(cm.xh_b, Xnt_q, tj, width=QT)
        for n in range(8):
            for k in range(8):
                b.ld(w1b[:, k * 512:(k + 1) * 512], A["p_l2f_w1T"][k * 128:(k + 1) * 128, n * 512:(n + 1) * 512])
                b.ld(w2b[:, k * 512:(k + 1) * 512], A["p_l2f_w2T"][k * 128:(k + 1) * 128, n * 512:(n + 1) * 512])
            for m in range(NQ):
                for k in range(8):
                    b.mm(cm.ps0[:, :], Xnt_q[:, k * QT + m * 128: k * QT + m * 128 + 128],
                         w1b[:, k * 512:(k + 1) * 512], first=(k == 0), last=(k == 7))
                for k in range(8):
                    b.mm(cm.ps1[:, :], Xnt_q[:, k * QT + m * 128: k * QT + m * 128 + 128],
                         w2b[:, k * 512:(k + 1) * 512], first=(k == 0), last=(k == 7))
                b.s(lambda e: e.activation(stg1[:, :], cm.ps0[:, :], AF.Silu))
                b.v(lambda e: e.tensor_mul(stgb[:, :], stg1[:, :], cm.ps1[:, :]))
                for j in range(4):
                    b.tr(cm.ps_tb[:, j * 128:(j + 1) * 128], stgb[:, j * 128:(j + 1) * 128], cm.idb[:])
                dv = Ht[:, :].rearrange("p (c t) -> p c t", c=32)[:, n * 4:(n + 1) * 4, m * 128:(m + 1) * 128]
                sv = cm.ps_tb[:, 0:512].rearrange("p (c t) -> p c t", c=4)
                b.v(lambda e, dv=dv, sv=sv: e.tensor_copy(dv, sv))
        for (wname, src, dst) in (("p_l2f_h1T", Ht, H1t), ("p_l2f_h2T", H1t, Ht)):
            for n in range(8):
                for k in range(32):
                    b.ld(wbuf[:, k * 512:(k + 1) * 512], A[wname][k * 128:(k + 1) * 128, n * 512:(n + 1) * 512])
                for m in range(NQ):
                    for k in range(32):
                        b.mm(cm.ps0[:, :], src[:, k * QT + m * 128: k * QT + m * 128 + 128],
                             wbuf[:, k * 512:(k + 1) * 512], first=(k == 0), last=(k == 31))
                    b.s(lambda e: e.activation(stgb[:, :], cm.ps0[:, :], AF.Gelu))
                    for j in range(4):
                        b.tr(cm.ps_tb[:, j * 128:(j + 1) * 128], stgb[:, j * 128:(j + 1) * 128], cm.idb[:])
                    dv = dst[:, :].rearrange("p (c t) -> p c t", c=32)[:, n * 4:(n + 1) * 4, m * 128:(m + 1) * 128]
                    sv = cm.ps_tb[:, 0:512].rearrange("p (c t) -> p c t", c=4)
                    b.v(lambda e, dv=dv, sv=sv: e.tensor_copy(dv, sv))
        for n in range(2):
            for k in range(32):
                b.ld(wbuf[:, k * 512:(k + 1) * 512], A["p_l2f_w3T"][k * 128:(k + 1) * 128, n * 512:(n + 1) * 512])
            for m in range(NQ):
                for k in range(32):
                    b.mm(cm.ps1[:, :], Ht[:, k * QT + m * 128: k * QT + m * 128 + 128],
                         wbuf[:, k * 512:(k + 1) * 512], first=(k == 0), last=(k == 31))
                b.ld(cm.xt[:, :], xin[tok0 + m * 128: tok0 + (m + 1) * 128, :])
                b.v(lambda e, n=n, m=m: e.scalar_tensor_tensor(
                    ysb2[:, m * 1024 + n * 512: m * 1024 + (n + 1) * 512], cm.ps1[:, :], 1.0,
                    cm.xt[:, n * 512:(n + 1) * 512], ALU.mult, ALU.add))
                if n == 1:
                    b.stdma(y[tok0 + m * 128: tok0 + (m + 1) * 128, :],
                            ysb2[:, m * 1024:(m + 1) * 1024])



L3_SCALE = 1.0 / math.sqrt(96.0)
HCH = [(0, 3), (3, 3), (6, 3), (9, 3), (12, 3), (15, 1)]


def l3_nope_slice(buf, h, ti_cols, width=T):
    ci, off = h // 3, (h % 3) * 32
    return buf[off:off + 32, ci * width + ti_cols[0]: ci * width + ti_cols[1]]


def l3_rope_slice(buf, h, ti_cols, width=T):
    ci, off = h // 2, (h % 2) * 64
    return buf[off:off + 64, ci * width + ti_cols[0]: ci * width + ti_cols[1]]


class L3Base:
    def __init__(self, bld, A, want_xtm=False, want_lo=False, pwidth=2048):
        b = self.b = bld
        self.A = A
        cm = self.cm = Common(bld, A)
        self.Xnt = b.sbuf([128, 8 * T], BF16)
        self.Xtm = b.sbuf([128, NT * 1024], BF16) if want_xtm else None
        self.Xlo = b.sbuf([128, 8 * T], BF16) if want_lo else None
        self.nqT = b.sbuf([96, T], BF16)
        self.cosd = b.sbuf([64, T], F32)
        self.sind = b.sbuf([64, T], F32)
        self.swap = b.sbuf([64, 64], F32)
        self.wdqn = b.sbuf([96, 512], BF16)
        self.wdqr = b.sbuf([96, 1024], BF16)
        self.wcq = b.sbuf([128, 8 * 96], BF16)
        self.gq = b.sbuf([128, 96], F32)
        self.Qn_q = b.sbuf([128, 6 * 128], BF16)
        self.Qr_q = b.sbuf([128, 8 * 128], BF16)
        self.qf = b.sbuf([64, 512], F32)
        self.P_sb = b.sbuf([128, pwidth], BF16)
        self.Pt_sb = b.sbuf([128, pwidth], BF16)
        self.rs = b.sbuf([128, 1], F32)
        self.rsg = b.sbuf([128, 1], F32)
        self.gb = b.sbuf([128, 3], F32)
        self.Osb = b.sbuf([128, 512], F32)
        xin = A["xin"]
        b.ld(self.cosd[:, :], A["cos3d"][:, :])
        b.ld(self.sind[:, :], A["sin3sgn"][:, :])
        b.ld(self.swap[:, :], A["swap64"][:, :])
        b.ld(self.gq[:, :], A["p_l3_gq"][:, :])
        stream_wT(b, cm, self.wcq, A["p_l3_wcq"], 1024, 96)
        b.ld(self.wdqn[:, :], A["p_l3_wdqn"][:, :])
        b.ld(self.wdqr[:, :], A["p_l3_wdqr"][:, :])
        for ti in range(NT):
            cm.ln_tile(xin, ti)
            cm.tile_to_fmajor(cm.xh_b, self.Xnt, ti)
            if want_xtm:
                b.copy(self.Xtm[:, ti * 1024:(ti + 1) * 1024], cm.xh_b[:, :])
            if want_lo:
                b.copy(cm.sq[:, :], cm.xh_b[:, :])
                b.v(lambda e: e.tensor_sub(cm.sq[:, :], cm.xc[:, :], cm.sq[:, :]))
                b.v(lambda e: e.tensor_scalar_mul(cm.xc[:, :], cm.xc[:, :], cm.col2[:]))
                b.copy(cm.sq[:, :], cm.xh_b[:, :])
                b.v(lambda e: e.tensor_sub(cm.sq[:, :], cm.xc[:, :], cm.sq[:, :]))
                b.copy(cm.xh_b[:, :], cm.sq[:, :])
                cm.tile_to_fmajor(cm.xh_b, self.Xlo, ti)
        for ti in range(NT):
            for k in range(8):
                b.mm(cm.ps0[:, 0:96], self.Xnt[:, k * T + ti * 128: k * T + ti * 128 + 128],
                     self.wcq[:, k * 96:(k + 1) * 96], first=(k == 0), last=(k == 7))
            b.copy(cm.xc[:, 0:96], cm.ps0[:, 0:96])
            b.v(lambda e: e.tensor_mul(cm.sq[:, 0:96], cm.xc[:, 0:96], cm.xc[:, 0:96]))
            b.v(lambda e: e.reduce_sum(cm.col0[:], cm.sq[:, 0:96], axis=AX.X))
            b.s(lambda e: e.activation(cm.col1[:], cm.col0[:], AF.Sqrt, bias=cm.c_eps6[:], scale=1.0 / 96.0))
            b.v(lambda e: e.reciprocal(cm.col2[:], cm.col1[:]))
            b.v(lambda e: e.tensor_scalar_mul(cm.xc[:, 0:96], cm.xc[:, 0:96], cm.col2[:]))
            b.v(lambda e: e.tensor_mul(cm.xh_b[:, 0:96], cm.xc[:, 0:96], self.gq[:, :]))
            b.tr(cm.ps_tb[0:96, 0:128], cm.xh_b[:, 0:96], cm.idb[:])
            b.copy(self.nqT[0:96, ti * 128:(ti + 1) * 128], cm.ps_tb[0:96, 0:128])

    def build_q(self, qi):
        b, cm, A = self.b, self.cm, self.A
        for ci, (h0, nh) in enumerate(HCH):
            b.mm(cm.ps0[0:nh * 32, 0:128], self.wdqn[:, h0 * 32:(h0 + nh) * 32],
                 self.nqT[0:96, qi * 128:(qi + 1) * 128])
            b.v(lambda e, ci=ci, nh=nh: e.tensor_scalar_mul(
                self.Qn_q[0:nh * 32, ci * 128:(ci + 1) * 128], cm.ps0[0:nh * 32, 0:128], L3_SCALE))
        for h in range(16):
            b.mm(cm.ps1[0:64, 0:128], self.wdqr[:, h * 64:(h + 1) * 64],
                 self.nqT[0:96, qi * 128:(qi + 1) * 128])
            b.v(lambda e, h=h: e.tensor_scalar_mul(self.qf[:, 0:128], cm.ps1[0:64, 0:128], L3_SCALE))
            self.rope_fm(self.qf, 128, qi * 128)
            ci, off = h // 2, (h % 2) * 64
            b.copy(self.Qr_q[off:off + 64, ci * 128:(ci + 1) * 128], self.qf[:, 0:128])

    def rope_fm(self, buf, w, tok0):
        b, cm = self.b, self.cm
        b.mm(cm.ps2[0:64, 0:w], self.swap[:, :], buf[0:64, 0:w])
        b.v(lambda e: e.tensor_mul(cm.sq[0:64, 0:w], buf[0:64, 0:w], self.cosd[:, tok0:tok0 + w]))
        b.v(lambda e: e.tensor_mul(cm.xc[0:64, 0:w], cm.ps2[0:64, 0:w], self.sind[:, tok0:tok0 + w]))
        b.v(lambda e: e.tensor_add(buf[0:64, 0:w], cm.sq[0:64, 0:w], cm.xc[0:64, 0:w]))


def _l3_attend(base, qi, kn_buf, kr_buf, kr_shared, v_buf, kvalid, causal, width, g_col, first_prog):
    b, cm, A = base.b, base.cm, base.A
    nch = (kvalid + 127) // 128
    for h in range(16):
        for sc in range((kvalid + 511) // 512):
            w = min(512, kvalid - sc * 512)
            b.mm(cm.ps_s[:, sc * 512: sc * 512 + w],
                 base.Qn_q[(h % 3) * 32:(h % 3) * 32 + 32, (h // 3) * 128:(h // 3) * 128 + 128],
                 l3_nope_slice(kn_buf, h, (sc * 512, sc * 512 + w), width), first=True, last=False)
            if kr_shared:
                off = (h % 2) * 64
                rrhs = kr_buf[off:off + 64, sc * 512: sc * 512 + w]
            else:
                rrhs = l3_rope_slice(kr_buf, h, (sc * 512, sc * 512 + w), width)
            b.mm(cm.ps_s[:, sc * 512: sc * 512 + w],
                 base.Qr_q[(h % 2) * 64:(h % 2) * 64 + 64, (h // 2) * 128:(h // 2) * 128 + 128],
                 rrhs, first=False, last=True)
        cm.softmax(kvalid, base.P_sb, masked_from=(qi * 128 if causal else None), rs_out=base.rs)
        cm.transpose_P(base.P_sb, base.Pt_sb, nch)
        for kc in range(nch):
            b.mm(cm.ps0[:, 0:32], base.Pt_sb[:, kc * 128:(kc + 1) * 128],
                 v_buf[:, kc * 512 + h * 32: kc * 512 + h * 32 + 32], first=(kc == 0), last=(kc == nch - 1))
        b.v(lambda e: e.tensor_mul(base.rsg[:, :], base.rs[:, :], g_col))
        if first_prog:
            b.v(lambda e, h=h: e.tensor_scalar_mul(base.Osb[:, h * 32:(h + 1) * 32], cm.ps0[:, 0:32], base.rsg[:]))
        else:
            b.v(lambda e, h=h: e.scalar_tensor_tensor(
                base.Osb[:, h * 32:(h + 1) * 32], cm.ps0[:, 0:32], base.rsg[:],
                base.Osb[:, h * 32:(h + 1) * 32], ALU.mult, ALU.add))


def emit_l3a1(bld, A):
    b = bld
    base = L3Base(bld, A)
    cm = base.cm
    Kn = b.sbuf([128, 6 * T], BF16)
    Krs = b.sbuf([128, T], BF16)
    Vb = b.sbuf([128, NT * 512], BF16)
    ckvT = b.sbuf([32, T], BF16)
    wsml = b.sbuf([128, 8 * 64], BF16)
    gkv = b.sbuf([128, 32], F32)
    wdkn = b.sbuf([32, 512], BF16)
    wdv = b.sbuf([32, 512], BF16)
    tg_b = b.sbuf([128, 48], BF16)
    gsc = b.sbuf([4, 4], F32)
    grow = b.sbuf([1, 8], F32)
    xin = A["xin"]
    b.ld(gkv[:, :], A["p_l3_gkv"][:, :])
    b.ld(wdkn[:, :], A["p_l3_wdkn"][:, :])
    b.ld(wdv[:, :], A["p_l3_wdv"][:, :])
    stream_wT(b, cm, wsml, A["p_l3_wckv"], 1024, 32)
    for ti in range(NT):
        for k in range(8):
            b.mm(cm.ps0[:, 0:32], base.Xnt[:, k * T + ti * 128: k * T + ti * 128 + 128],
                 wsml[:, k * 32:(k + 1) * 32], first=(k == 0), last=(k == 7))
        b.copy(cm.xc[:, 0:32], cm.ps0[:, 0:32])
        b.v(lambda e: e.tensor_mul(cm.sq[:, 0:32], cm.xc[:, 0:32], cm.xc[:, 0:32]))
        b.v(lambda e: e.reduce_sum(cm.col0[:], cm.sq[:, 0:32], axis=AX.X))
        b.s(lambda e: e.activation(cm.col1[:], cm.col0[:], AF.Sqrt, bias=cm.c_eps6[:], scale=1.0 / 32.0))
        b.v(lambda e: e.reciprocal(cm.col2[:], cm.col1[:]))
        b.v(lambda e: e.tensor_scalar_mul(cm.xc[:, 0:32], cm.xc[:, 0:32], cm.col2[:]))
        b.v(lambda e: e.tensor_mul(cm.xh_b[:, 0:32], cm.xc[:, 0:32], gkv[:, :]))
        b.tr(cm.ps_tb[0:32, 0:128], cm.xh_b[:, 0:32], cm.idb[:])
        b.copy(ckvT[:, ti * 128:(ti + 1) * 128], cm.ps_tb[0:32, 0:128])
    for ci, (h0, nh) in enumerate(HCH):
        for tc in range(4):
            b.mm(cm.ps0[0:nh * 32, :], wdkn[:, h0 * 32:(h0 + nh) * 32], ckvT[:, tc * 512:(tc + 1) * 512])
            b.copy(Kn[0:nh * 32, ci * T + tc * 512: ci * T + (tc + 1) * 512], cm.ps0[0:nh * 32, :])
    for ti in range(NT):
        b.mm(cm.ps0[:, :], ckvT[:, ti * 128:(ti + 1) * 128], wdv[:, :])
        b.copy(Vb[:, ti * 512:(ti + 1) * 512], cm.ps0[:, :])
    stream_wT(b, cm, wsml, A["p_l3_wkrope"], 1024, 64)
    for tc in range(4):
        for k in range(8):
            b.mm(cm.ps1[0:64, :], wsml[:, k * 64:(k + 1) * 64],
                 base.Xnt[:, k * T + tc * 512: k * T + (tc + 1) * 512], first=(k == 0), last=(k == 7))
        b.v(lambda e: e.tensor_scalar_mul(base.qf[:, :], cm.ps1[0:64, :], 1.0 / 16.0))
        base.rope_fm(base.qf, 512, tc * 512)
        b.copy(Krs[0:64, tc * 512:(tc + 1) * 512], base.qf[:, :])
        b.copy(Krs[64:128, tc * 512:(tc + 1) * 512], base.qf[:, :])
    stream_wT(b, cm, wsml, A["p_l3_wgate"], 1024, 3)
    for ti in range(NT):
        for k in range(8):
            b.mm(cm.ps2[:, 0:3], base.Xnt[:, k * T + ti * 128: k * T + ti * 128 + 128],
                 wsml[:, k * 3:(k + 1) * 3], first=(k == 0), last=(k == 7))
        b.copy(tg_b[:, ti * 3:(ti + 1) * 3], cm.ps2[:, 0:3])
    for ti in range(NT):
        b.mm(cm.ps1[0:3, 0:1], tg_b[:, ti * 3:(ti + 1) * 3], cm.ocol_b[:], first=(ti == 0), last=(ti == NT - 1))
    b.copy(gsc[0:3, 0:1], cm.ps1[0:3, 0:1])
    b.tr(cm.ps0[0:1, 0:3], gsc[0:3, 0:1], cm.idf[0:3, 0:3])
    b.copy(grow[0:1, 0:3], cm.ps0[0:1, 0:3])
    b.v(lambda e: e.tensor_scalar_mul(grow[0:1, 0:3], grow[0:1, 0:3], 1.0 / T))
    b.v(lambda e: e.tensor_reduce(grow[0:1, 3:4], grow[0:1, 0:3], axis=AX.X, op=ALU.max))
    b.v(lambda e: e.tensor_scalar_mul(grow[0:1, 4:5], grow[0:1, 3:4], -1.0))
    b.s(lambda e: e.activation(grow[0:1, 5:8], grow[0:1, 0:3], AF.Exp, bias=grow[0:1, 4:5]))
    b.v(lambda e: e.reduce_sum(grow[0:1, 3:4], grow[0:1, 5:8], axis=AX.X))
    b.v(lambda e: e.reciprocal(grow[0:1, 4:5], grow[0:1, 3:4]))
    b.v(lambda e: e.tensor_scalar_mul(grow[0:1, 5:8], grow[0:1, 5:8], grow[0:1, 4:5]))
    for i in range(3):
        b.mm(cm.ps0[:, 0:1], cm.orow_f[:], grow[0:1, 5 + i:6 + i])
        b.copy(base.gb[:, i:i + 1], cm.ps0[:, 0:1])
    b.stdma(A["gates"][:, :], base.gb[:, :])
    for qi in range(NT):
        base.build_q(qi)
        _l3_attend(base, qi, Kn, Krs, True, Vb, (qi + 1) * 128, True, T,
                   base.gb[:, 0:1], first_prog=True)
        b.stdma(A["o_out"][qi * 128:(qi + 1) * 128, :], base.Osb[:, :])


def emit_l3a2(bld, A):
    b = bld
    base = L3Base(bld, A, want_xtm=True, want_lo=True, pwidth=256)
    cm = base.cm
    wimp_hi = b.sbuf([128, 8], BF16)
    wimp_lo = b.sbuf([128, 8], BF16)
    s_row = b.sbuf([1, 2048], F32)
    s_pc = b.sbuf([128, 16], F32)
    cmp_sb = b.sbuf([128, 16], F32)
    m_pc = b.sbuf([128, 16], F32)
    m_b = b.sbuf([128, 16], BF16)
    R_pc = b.sbuf([128, 16], F32)
    crow = b.sbuf([1, 2048], F32)
    mrow = s_row
    scal = b.sbuf([1, 8], F32)
    tsb = b.sbuf([128, 512], BF16)
    iota_sb = b.sbuf([128, 256], F32)
    Gt = b.sbuf([128, 16 * 256], BF16)
    sel_tm = b.sbuf([128, 2 * 1024], BF16)
    selT = b.sbuf([128, 8 * 256], BF16)
    Ksn = b.sbuf([128, 6 * 256], BF16)
    Ksr = b.sbuf([128, 8 * 256], BF16)
    Vs = b.sbuf([128, 2 * 512], BF16)
    wstr8 = b.sbuf([128, 8 * 512], BF16)
    b.ld(iota_sb[:, :], A["iota_bc"][:, :])
    b.ld(wimp_hi[:, :], A["p_l3_wimp_hi"][:, :])
    b.ld(wimp_lo[:, :], A["p_l3_wimp_lo"][:, :])
    b.ld(base.gb[:, :], A["gates"][:, :])
    for tc in range(4):
        prods = [(wimp_hi, base.Xnt), (wimp_hi, base.Xlo), (wimp_lo, base.Xnt)]
        n = 0
        for (wv, xv) in prods:
            for k in range(8):
                b.mm(cm.ps_s[0:1, tc * 512:(tc + 1) * 512], wv[:, k:k + 1],
                     xv[:, k * T + tc * 512: k * T + (tc + 1) * 512],
                     first=(n == 0), last=(n == 23))
                n += 1
    b.copy(s_row[0:1, :], cm.ps_s[0:1, 0:2048])
    for c in range(16):
        b.tr(cm.ps0[:, 0:1], s_row[0:1, c * 128:(c + 1) * 128], cm.idf[0:1, 0:1])
        b.copy(s_pc[:, c:c + 1], cm.ps0[:, 0:1])
    b.v(lambda e: e.tensor_reduce(cm.col0[:], s_pc[:, :], axis=AX.X, op=ALU.max))
    b.tr(cm.ps0[0:1, 0:128], cm.col0[:, 0:1], cm.idf[:, :])
    b.copy(s_row[0:1, 0:128], cm.ps0[0:1, 0:128])
    b.v(lambda e: e.tensor_reduce(scal[0:1, 1:2], s_row[0:1, 0:128], axis=AX.X, op=ALU.max))
    b.v(lambda e: e.tensor_reduce(cm.col0[:], s_pc[:, :], axis=AX.X, op=ALU.min))
    b.tr(cm.ps0[0:1, 0:128], cm.col0[:, 0:1], cm.idf[:, :])
    b.copy(s_row[0:1, 0:128], cm.ps0[0:1, 0:128])
    b.v(lambda e: e.tensor_reduce(scal[0:1, 0:1], s_row[0:1, 0:128], axis=AX.X, op=ALU.min))
    b.v(lambda e: e.tensor_scalar_add(scal[0:1, 0:1], scal[0:1, 0:1], -1.0))
    b.v(lambda e: e.tensor_scalar_add(scal[0:1, 1:2], scal[0:1, 1:2], 1.0))
    for it in range(46):
        b.v(lambda e: e.tensor_add(scal[0:1, 2:3], scal[0:1, 0:1], scal[0:1, 1:2]))
        b.v(lambda e: e.tensor_scalar_mul(scal[0:1, 2:3], scal[0:1, 2:3], 0.5))
        b.mm(cm.ps1[:, 0:1], cm.orow_f[:], scal[0:1, 2:3])
        b.copy(cm.col3[:], cm.ps1[:, 0:1])
        b.v(lambda e: e.tensor_scalar(cmp_sb[:, :], s_pc[:, :], cm.col3[:], None, ALU.is_ge))
        b.v(lambda e: e.reduce_sum(cm.col0[:], cmp_sb[:, :], axis=AX.X))
        b.mm(cm.ps1[0:1, 1:2], cm.col0[:, 0:1], cm.ocol_f[:])
        b.v(lambda e: e.tensor_scalar(scal[0:1, 3:4], cm.ps1[0:1, 1:2], 255.5, None, ALU.is_ge))
        b.v(lambda e: e.tensor_sub(scal[0:1, 4:5], scal[0:1, 2:3], scal[0:1, 0:1]))
        b.v(lambda e: e.tensor_mul(scal[0:1, 4:5], scal[0:1, 4:5], scal[0:1, 3:4]))
        b.v(lambda e: e.tensor_add(scal[0:1, 0:1], scal[0:1, 0:1], scal[0:1, 4:5]))
        b.v(lambda e: e.tensor_sub(scal[0:1, 4:5], scal[0:1, 1:2], scal[0:1, 2:3]))
        b.v(lambda e: e.tensor_scalar(scal[0:1, 5:6], scal[0:1, 3:4], -1.0, 1.0, ALU.mult, ALU.add))
        b.v(lambda e: e.tensor_mul(scal[0:1, 4:5], scal[0:1, 4:5], scal[0:1, 5:6]))
        b.v(lambda e: e.tensor_sub(scal[0:1, 1:2], scal[0:1, 1:2], scal[0:1, 4:5]))
    b.mm(cm.ps1[:, 0:1], cm.orow_f[:], scal[0:1, 0:1])
    b.copy(cm.col3[:], cm.ps1[:, 0:1])
    b.v(lambda e: e.tensor_scalar(m_pc[:, :], s_pc[:, :], cm.col3[:], None, ALU.is_ge))
    b.copy(m_b[:, :], m_pc[:, :])
    for tc in range(4):
        for c in range(16):
            t_ld = b.ld(tsb[:, :], A["tsmask"][c * 128:(c + 1) * 128, tc * 512:(tc + 1) * 512])
            b.t(lambda e, tc=tc, c=c: e.matmul(
                cm.ps_s[0:1, tc * 512:(tc + 1) * 512], m_b[:, c:c + 1], tsb[:, :],
                start=(c == 0), stop=(c == 15)), deps=[t_ld])
    b.copy(crow[0:1, :], cm.ps_s[0:1, 0:2048])
    for c in range(16):
        b.tr(cm.ps0[0:1, 0:128], m_pc[:, c:c + 1], cm.idf[:, :])
        b.copy(mrow[0:1, c * 128:(c + 1) * 128], cm.ps0[0:1, 0:128])
    b.v(lambda e: e.tensor_scalar_add(crow[0:1, :], crow[0:1, :], 1.0))
    b.v(lambda e: e.tensor_mul(crow[0:1, :], crow[0:1, :], mrow[0:1, :]))
    b.v(lambda e: e.tensor_scalar_add(crow[0:1, :], crow[0:1, :], -1.0))
    for c in range(16):
        b.tr(cm.ps0[:, 0:1], crow[0:1, c * 128:(c + 1) * 128], cm.idf[0:1, 0:1])
        b.copy(R_pc[:, c:c + 1], cm.ps0[:, 0:1])
    for c in range(16):
        b.v(lambda e, c=c: e.tensor_tensor(
            Gt[:, c * 256:(c + 1) * 256], R_pc[:, c:c + 1].broadcast_to([128, 256]), iota_sb[:, :],
            ALU.is_equal))
    for mh in range(2):
        for n2 in range(2):
            for c in range(16):
                b.mm(cm.ps0[:, :], Gt[:, c * 256 + mh * 128: c * 256 + mh * 128 + 128],
                     base.Xtm[:, c * 1024 + n2 * 512: c * 1024 + (n2 + 1) * 512],
                     first=(c == 0), last=(c == 15))
            b.copy(sel_tm[:, mh * 1024 + n2 * 512: mh * 1024 + (n2 + 1) * 512], cm.ps0[:, :])
    for mh in range(2):
        for c in range(8):
            b.tr(cm.ps_tb[:, 0:128], sel_tm[:, mh * 1024 + c * 128: mh * 1024 + (c + 1) * 128], cm.idb[:])
            b.copy(selT[:, c * 256 + mh * 128: c * 256 + (mh + 1) * 128], cm.ps_tb[:, 0:128])
    for ci, (h0, nh) in enumerate(HCH):
        for k in range(8):
            b.ld(wstr8[:, k * 512: k * 512 + nh * 32].rearrange("p (h d) -> p h d", h=nh),
                 A["p_l3_wselk"][k * 128:(k + 1) * 128, :].rearrange("p (h d) -> p h d", h=16)[:, h0:h0 + nh, 0:32])
        for k in range(8):
            b.mm(cm.ps0[0:nh * 32, 0:256], wstr8[:, k * 512: k * 512 + nh * 32],
                 selT[:, k * 256:(k + 1) * 256], first=(k == 0), last=(k == 7))
        b.copy(Ksn[0:nh * 32, ci * 256:(ci + 1) * 256], cm.ps0[0:nh * 32, 0:256])
    for h in range(16):
        for k in range(8):
            b.ld(wstr8[:, k * 512: k * 512 + 64],
                 A["p_l3_wselk"][k * 128:(k + 1) * 128, h * 96 + 32:(h + 1) * 96])
        for k in range(8):
            b.mm(cm.ps1[0:64, 0:256], wstr8[:, k * 512: k * 512 + 64],
                 selT[:, k * 256:(k + 1) * 256], first=(k == 0), last=(k == 7))
        b.copy(base.qf[:, 0:256], cm.ps1[0:64, 0:256])
        base.rope_fm(base.qf, 256, 0)
        b.copy(Ksr[(h % 2) * 64:(h % 2) * 64 + 64, (h // 2) * 256:(h // 2 + 1) * 256], base.qf[:, 0:256])
    for k in range(8):
        b.ld(wstr8[:, k * 512:(k + 1) * 512], A["p_l3_wselv"][k * 128:(k + 1) * 128, :])
    for mh in range(2):
        for k in range(8):
            b.mm(cm.ps0[:, :], selT[:, k * 256 + mh * 128: k * 256 + mh * 128 + 128],
                 wstr8[:, k * 512:(k + 1) * 512], first=(k == 0), last=(k == 7))
        b.copy(Vs[:, mh * 512:(mh + 1) * 512], cm.ps0[:, :])
    if "d_m" in A:
        b.stdma(A["d_m"][:, :], m_pc[:, :])
        b.stdma(A["d_R"][:, :], R_pc[:, :])
        b.stdma(A["d_s"][:, :], s_row[0:1, :])
        b.stdma(A["d_sel"][:, :], sel_tm[:, :])
        b.stdma(A["d_ksn"][:, :], Ksn[:, :])
        b.stdma(A["d_ksr"][:, :], Ksr[:, :])
        b.stdma(A["d_vs"][:, :], Vs[:, :])
        b.stdma(A["d_crow"][:, :], crow[0:1, :])
    for qi in range(NT):
        base.build_q(qi)
        b.ld(base.Osb[:, :], A["o_in"][qi * 128:(qi + 1) * 128, :])
        _l3_attend(base, qi, Ksn, Ksr, False, Vs, 256, False, 256,
                   base.gb[:, 1:2], first_prog=False)
        b.stdma(A["o_out"][qi * 128:(qi + 1) * 128, :], base.Osb[:, :])


def emit_l3a3(bld, A):
    b = bld
    base = L3Base(bld, A)
    cm = base.cm
    Kn = b.sbuf([128, 6 * T], BF16)
    Kwr = b.sbuf([128, 8 * T], BF16)
    Vb = b.sbuf([128, NT * 512], BF16)
    wstr8 = b.sbuf([128, 8 * 512], BF16)
    wproj = b.sbuf([128, 4 * 1024], BF16)
    Ob = b.sbuf([128, 512], BF16)
    Ot = b.sbuf([128, 4 * 128], BF16)
    b.ld(base.gb[:, :], A["gates"][:, :])
    stream_wT(b, cm, wproj, A["p_l3_wproj"], 512, 1024)
    for ci, (h0, nh) in enumerate(HCH):
        for k in range(8):
            b.ld(wstr8[:, k * 512: k * 512 + nh * 32].rearrange("p (h d) -> p h d", h=nh),
                 A["p_l3_wwink"][k * 128:(k + 1) * 128, :].rearrange("p (h d) -> p h d", h=16)[:, h0:h0 + nh, 0:32])
        for tc in range(4):
            for k in range(8):
                b.mm(cm.ps0[0:nh * 32, :], wstr8[:, k * 512: k * 512 + nh * 32],
                     base.Xnt[:, k * T + tc * 512: k * T + (tc + 1) * 512], first=(k == 0), last=(k == 7))
            b.copy(Kn[0:nh * 32, ci * T + tc * 512: ci * T + (tc + 1) * 512], cm.ps0[0:nh * 32, :])
    for h in range(16):
        for k in range(8):
            b.ld(wstr8[:, k * 512: k * 512 + 64],
                 A["p_l3_wwink"][k * 128:(k + 1) * 128, h * 96 + 32:(h + 1) * 96])
        for tc in range(4):
            for k in range(8):
                b.mm(cm.ps1[0:64, :], wstr8[:, k * 512: k * 512 + 64],
                     base.Xnt[:, k * T + tc * 512: k * T + (tc + 1) * 512], first=(k == 0), last=(k == 7))
            b.copy(base.qf[:, :], cm.ps1[0:64, :])
            base.rope_fm(base.qf, 512, tc * 512)
            b.copy(Kwr[(h % 2) * 64:(h % 2) * 64 + 64, (h // 2) * T + tc * 512:(h // 2) * T + (tc + 1) * 512],
                   base.qf[:, :])
    for k in range(8):
        b.ld(wstr8[:, k * 512:(k + 1) * 512], A["p_l3_wwinv"][k * 128:(k + 1) * 128, :])
    for ti in range(NT):
        for k in range(8):
            b.mm(cm.ps0[:, :], base.Xnt[:, k * T + ti * 128: k * T + ti * 128 + 128],
                 wstr8[:, k * 512:(k + 1) * 512], first=(k == 0), last=(k == 7))
        b.copy(Vb[:, ti * 512:(ti + 1) * 512], cm.ps0[:, :])
    xin, y = A["xin"], A["y"]
    for qi in range(NT):
        base.build_q(qi)
        b.ld(base.Osb[:, :], A["o_in"][qi * 128:(qi + 1) * 128, :])
        _l3_attend(base, qi, Kn, Kwr, False, Vb, (qi + 1) * 128, True, T,
                   base.gb[:, 2:3], first_prog=False)
        b.copy(Ob[:, :], base.Osb[:, :])
        for c in range(4):
            b.tr(cm.ps_tb[:, c * 128:(c + 1) * 128], Ob[:, c * 128:(c + 1) * 128], cm.idb[:])
        b.copy(Ot[:, :], cm.ps_tb[:, 0:512])
        b.ld(cm.xt[:], xin[qi * 128:(qi + 1) * 128, :])
        for n in range(2):
            for k in range(4):
                b.mm(cm.ps1[:, :], Ot[:, k * 128:(k + 1) * 128],
                     wproj[:, k * 1024 + n * 512: k * 1024 + n * 512 + 512], first=(k == 0), last=(k == 3))
            b.v(lambda e, n=n: e.scalar_tensor_tensor(
                cm.ysb[:, n * 512:(n + 1) * 512], cm.ps1[:, :], 1.0, cm.xt[:, n * 512:(n + 1) * 512],
                ALU.mult, ALU.add))
        b.stdma(y[qi * 128:(qi + 1) * 128, :], cm.ysb[:, :])



def emit_l3ffn(bld, A):
    b = bld
    cm = Common(bld, A)
    Xnt_t = b.sbuf([128, 1024], BF16)
    sWT = b.sbuf([128, 8 * 1024], BF16)
    xin, y = A["xin"], A["y"]
    stream_wT(b, cm, sWT, A["p_l3f_sWT"], 1024, 1024)
    for ti in range(NT):
        cm.ln_tile(xin, ti)
        cm.tile_to_fmajor(cm.xh_b, Xnt_t, 0, width=128)
        for n in range(2):
            for k in range(8):
                b.mm(cm.ps0[:, :], Xnt_t[:, k * 128:(k + 1) * 128],
                     sWT[:, k * 1024 + n * 512: k * 1024 + n * 512 + 512], first=(k == 0), last=(k == 7))
            b.v(lambda e, n=n: e.scalar_tensor_tensor(
                cm.ysb[:, n * 512:(n + 1) * 512], cm.ps0[:, :], 1.0, cm.xt[:, n * 512:(n + 1) * 512],
                ALU.mult, ALU.add))
        b.stdma(y[ti * 128:(ti + 1) * 128, :], cm.ysb[:, :])


_f3 = 1.0 / (10000.0 ** (np.arange(0, 64, 2, dtype=np.float32) / 64))
_a3 = np.arange(T, dtype=np.float32)[:, None] * _f3[None, :]
_c3 = np.cos(_a3).T.astype(np.float32)
_s3 = np.sin(_a3).T.astype(np.float32)
CONSTS["cos3d"] = np.ascontiguousarray(np.concatenate([_c3, _c3], axis=0))
CONSTS["sin3sgn"] = np.ascontiguousarray(np.concatenate([-_s3, _s3], axis=0))
_sw = np.zeros((64, 64), dtype=np.float32)
for _i in range(32):
    _sw[32 + _i, _i] = 1.0
    _sw[_i, 32 + _i] = 1.0
CONSTS["swap64"] = _sw
CONST_IO.update({"cos3d": ((64, T), F32), "sin3sgn": ((64, T), F32), "swap64": ((64, 64), F32)})

CONST_BLOCK = ["ident_b", "ident_f", "tril01", "ones_row_f", "ones_row_b",
               "ones_col_f", "ones_col_b"]


def blk_io(prep_names, const_names=(), extra=()):
    io = {"xin": ((T, C), F32, "ExternalInput")}
    for n in prep_names:
        io[n] = (PREP_OUT[n][0], PREP_OUT[n][1], "ExternalInput")
    io.update(io_inputs(list(const_names) + CONST_BLOCK))
    for (nm, shape, dt, kind) in extra:
        io[nm] = (shape, dt, kind)
    return io


Y_OUT = ("y", (T, C), F32, "ExternalOutput")

BLOCK_DEFS = [
    ("l0a", emit_l0attn, ["p_l0_wkvd", "p_l0_wkvu", "p_l0_wq", "p_l0_wout"], [], [Y_OUT]),
    ("l0f", emit_l0ffn, ["p_l0f_WT", "p_l0f_daT", "p_l0f_ldT", "p_l0f_luT", "p_l0f_uaT", "p_l0f_b"], [], [Y_OUT]),
    ("l1a", emit_l1attn, ["p_l1_WqT", "p_l1_WkT", "p_l1_WvT", "p_l1_WoT", "p_l1_lam", "p_l1_lnw", "p_l1_lnb"],
     ["cos1", "sin1"], [Y_OUT]),
    ("l1f", emit_l1ffn, ["p_l1f_WrT", "p_l1f_WiT", "p_l1f_WiTn", "p_l1f_br", "p_l1f_bi"], [], [Y_OUT]),
    ("l2a", emit_l2attn, ["p_l2_WqT", "p_l2_WpT", "p_l2_bp"], [], [Y_OUT]),
    ("l2f", emit_l2ffn, ["p_l2f_w1T", "p_l2f_w2T", "p_l2f_h1T", "p_l2f_h2T", "p_l2f_w3T"], [], [Y_OUT]),
    ("l3a1", emit_l3a1,
     ["p_l3_wcq", "p_l3_gq", "p_l3_wdqn", "p_l3_wdqr", "p_l3_wckv", "p_l3_gkv",
      "p_l3_wdkn", "p_l3_wdv", "p_l3_wkrope", "p_l3_wgate"],
     ["cos3d", "sin3sgn", "swap64"],
     [("gates", (128, 3), F32, "ExternalOutput"), ("o_out", (T, 512), F32, "ExternalOutput")]),
    ("l3a2", emit_l3a2,
     ["p_l3_wcq", "p_l3_gq", "p_l3_wdqn", "p_l3_wdqr", "p_l3_wimp_hi", "p_l3_wimp_lo",
      "p_l3_wselk", "p_l3_wselv"],
     ["cos3d", "sin3sgn", "swap64", "iota_bc", "tsmask"],
     [("gates", (128, 3), F32, "ExternalInput"), ("o_in", (T, 512), F32, "ExternalInput"),
      ("o_out", (T, 512), F32, "ExternalOutput")]),
    ("l3a3", emit_l3a3,
     ["p_l3_wcq", "p_l3_gq", "p_l3_wdqn", "p_l3_wdqr", "p_l3_wwink", "p_l3_wwinv", "p_l3_wproj"],
     ["cos3d", "sin3sgn", "swap64"],
     [("gates", (128, 3), F32, "ExternalInput"), ("o_in", (T, 512), F32, "ExternalInput"), Y_OUT]),
    ("l3f", emit_l3ffn, ["p_l3f_sWT"], [], [Y_OUT]),
]

_CACHE = {}


def _prog(key, emit, io):
    if key not in _CACHE:
        _CACHE[key] = build_program(emit, io, key)
    return _CACHE[key]


def kernel(**inp):
    x = np.ascontiguousarray(np.asarray(inp["L0_attn_x"], dtype=np.float32)).reshape(16, T, C).copy()
    prep_im = {}
    for n in PREP_W_IN:
        prep_im[n] = np.ascontiguousarray(np.asarray(inp[n], dtype=np.float32).reshape(PREP_IN_SHAPES[n]))
    for cname in ["ident_b", "ident_f", "ones_row_f", "P1T", "P2", "Q1T", "Q2"]:
        prep_im[cname] = CONSTS[cname]
    nc = _prog("prep", emit_prep, prep_io())
    pres = run_bass_kernel_spmd(nc, [prep_im] * NCORE, core_ids=list(range(NCORE)))
    W = [pres.results[i] for i in range(NCORE)]

    o_part = [None] * 16
    gates = [None] * 16

    for (key, emit, pnames, cnames, extra) in BLOCK_DEFS:
        io = blk_io(pnames, cnames, extra)
        nc = _prog(key, emit, io)
        for half in range(2):
            in_maps = []
            for ci in range(NCORE):
                bi = ci * 2 + half
                im = {"xin": np.ascontiguousarray(x[bi])}
                for pn in pnames:
                    im[pn] = W[ci][pn]
                for cn in list(cnames) + CONST_BLOCK:
                    im[cn] = CONSTS[cn]
                if key == "l3a2":
                    im["gates"] = gates[bi]
                    im["o_in"] = o_part[bi]
                elif key == "l3a3":
                    im["gates"] = gates[bi]
                    im["o_in"] = o_part[bi]
                in_maps.append(im)
            res = run_bass_kernel_spmd(nc, in_maps, core_ids=list(range(NCORE)))
            for ci in range(NCORE):
                bi = ci * 2 + half
                r = res.results[ci]
                if "y" in r:
                    x[bi] = np.asarray(r["y"], dtype=np.float32)
                if "o_out" in r:
                    o_part[bi] = np.ascontiguousarray(np.asarray(r["o_out"], dtype=np.float32))
                if "gates" in r:
                    gates[bi] = np.ascontiguousarray(np.asarray(r["gates"], dtype=np.float32))
    return x.reshape(16, T, C)
